# version 07
# baseline (speedup 1.0000x reference)
import numpy as np
import ml_dtypes
from contextlib import ExitStack

import concourse.bass as bass
import concourse.mybir as mybir
from concourse.bass_utils import run_bass_kernel_spmd

F32 = mybir.dt.float32
BF16 = mybir.dt.bfloat16
AF = mybir.ActivationFunctionType
ALU = mybir.AluOpType
AX = mybir.AxisListType

NCORE = 8
T_CORE = 2048
HALO = 256
T_K = T_CORE + HALO
D = 2048
H = 16
DH = 128
NW = 8
DFF = 8192
CAP = 384
NJT = CAP // 128
KTOP = 1024
SHIFT = 20.0
RSHIFT = 16.0
NROUND = 10
NPROBE = 16

BF = ml_dtypes.bfloat16


def _f32(a):
    return np.ascontiguousarray(np.asarray(a), dtype=np.float32)


def _bf(a):
    return np.ascontiguousarray(np.asarray(a, dtype=np.float32).astype(BF))



class TEng:
    def __init__(self, t):
        self.t = t
        self.last = None

    def matmul(self, *a, **k):
        self.last = self.t.matmul(*a, **k)
        return self.last

    def transpose(self, *a, **k):
        self.last = self.t.transpose(*a, **k)
        return self.last

    def wait_ge(self, *a, **k):
        return self.t.wait_ge(*a, **k)

    def fin(self, sem, n=1):
        self.last.then_inc(sem, n)



class Eng:
    def __init__(self, e, ctr):
        self.e = e
        self.ctr = ctr
        self.last = None

    def __getattr__(self, name):
        f = getattr(self.e, name)
        if callable(f) and name not in ("wait_ge", "engine_nop"):
            def g(*a, **k):
                r = f(*a, **k)
                if hasattr(r, "then_inc"):
                    self.last = r
                return r
            return g
        return f

    def bar(self):
        self.last.then_inc(self.ctr.sem, 1)
        self.ctr.n += 1
        self.e.wait_ge(self.ctr.sem, self.ctr.n)


class Ctr:
    __slots__ = ("sem", "n")

    def __init__(self, sem):
        self.sem = sem
        self.n = 0


def build(debug=False, phases=4, ngroups=8, p1_upto=99):
    nc = bass.Bass("TRN2", target_bir_lowering=False)

    def din(name, shape, dt=F32):
        return nc.dram_tensor(name, shape, dt, kind="ExternalInput").ap()

    x_d = din("x", [T_CORE, D])
    xT_d = din("xT", [D, T_K])
    wq_d = din("wqT", [D, D])
    wk_d = din("wkT", [D, D])
    wv_d = din("wvT", [D, D])
    wo_d = din("woT", [D, D])
    w1_d = din("w1", [D, DFF])
    w2_d = din("w2", [DFF, D])
    wr_d = din("wr_rep", [128, D])
    b1_d = din("b1c", [DFF // 128, 128])
    b2_d = din("b2c", [D // 128, 128])
    br_d = din("br_rep", [128, 1])
    ident_d = din("ident_b", [128, 128], BF16)
    psw_d = din("psw_b", [128, 128], BF16)
    tri_d = din("trimask_b", [128, 128], BF16)
    w0p_d = din("w0prev_b", [128, 256], BF16)
    triu_d = din("triu_b", [128, 128], BF16)
    iota_d = din("iota384", [128, CAP])
    bm_d = din("bm16", [128, NPROBE])
    stepc_d = din("step_col", [128, 1])
    stepr_d = din("step_row", [1, NPROBE])
    repl_d = din("repl8", [8, 128])
    ones8_d = din("ones8", [8, 1])
    ones128_d = din("ones128", [128, 1])
    onesr_d = din("onesr_f32", [1, 128])
    c2q_d = din("c2q", [128, T_CORE], BF16)
    s2q_d = din("s2q", [128, T_CORE], BF16)
    c2k_d = din("c2k", [128, T_K], BF16)
    s2k_d = din("s2k", [128, T_K], BF16)

    out_d = nc.dram_tensor("out", [T_CORE, D], F32, kind="ExternalOutput").ap()

    dbg = {}
    if debug:
        for name, shape in [
            ("dbg_q0", [128, T_CORE]), ("dbg_k0", [128, T_K]),
            ("dbg_v0", [128, 256]), ("dbg_ao0", [128, T_CORE]),
            ("dbg_x1a", [128, D]), ("dbg_logit", [128, 16]),
            ("dbg_lh", [1, 8]), ("dbg_Q", [128, 16]),
            ("dbg_rw", [128, 16]), ("dbg_fx", [128, D]),
            ("dbg_h0", [128, CAP]), ("dbg_ol", [128, D]),
            ("dbg_cnt", [1, NPROBE]),
            ("dbg_psA", [128, 512]), ("dbg_psB", [128, 512]),
            ("dbg_probs", [128, 1024]), ("dbg_pT", [128, 1024]),
        ]:
            dbg[name] = nc.dram_tensor(name, shape, F32, kind="ExternalOutput").ap()

    ao_hbm = nc.dram_tensor("ao_hbm", [D, T_CORE], BF16)
    x1_hbm = nc.dram_tensor("x1_hbm", [T_CORE, D], F32)
    agin = nc.dram_tensor("agin", [1, T_CORE], F32)
    agout = nc.dram_tensor("agout", [4, T_CORE], F32)

    mil = {}
    dbg_written = set()
    guards = {}

    with ExitStack() as ctx:
        E = ctx.enter_context

        ident_s = E(nc.sbuf_tensor("ident_s", [128, 128], BF16))
        psw_s = E(nc.sbuf_tensor("psw_s", [128, 128], BF16))
        tri_s = E(nc.sbuf_tensor("tri_s", [128, 128], BF16))
        w0p_s = E(nc.sbuf_tensor("w0p_s", [128, 256], BF16))
        triu_s = E(nc.sbuf_tensor("triu_s", [128, 128], BF16))
        iota_s = E(nc.sbuf_tensor("iota_s", [128, CAP], F32))
        bm_s = E(nc.sbuf_tensor("bm_s", [128, NPROBE], F32))
        stepc_s = E(nc.sbuf_tensor("stepc_s", [128, 1], F32))
        stepr_s = E(nc.sbuf_tensor("stepr_s", [1, NPROBE], F32))
        repl_s = E(nc.sbuf_tensor("repl_s", [8, 128], F32))
        ones8_s = E(nc.sbuf_tensor("ones8_s", [8, 1], F32))
        ones128_s = E(nc.sbuf_tensor("ones128_s", [128, 1], F32))
        onesr_s = E(nc.sbuf_tensor("onesr_s", [1, 128], F32))
        c2q_s = E(nc.sbuf_tensor("c2q_s", [128, T_CORE], BF16))
        s2q_s = E(nc.sbuf_tensor("s2q_s", [128, T_CORE], BF16))
        c2k_s = E(nc.sbuf_tensor("c2k_s", [128, T_K], BF16))
        s2k_s = E(nc.sbuf_tensor("s2k_s", [128, T_K], BF16))
        b1_s = [E(nc.sbuf_tensor(f"b1_s{i}", [128, 1], F32)) for i in range(DFF // 128)]
        b2_s = [E(nc.sbuf_tensor(f"b2_s{i}", [128, 1], F32)) for i in range(D // 128)]
        br_s = E(nc.sbuf_tensor("br_s", [128, 1], F32))
        wr_f = E(nc.sbuf_tensor("wr_f", [128, D], F32))
        zero_c = E(nc.sbuf_tensor("zero_c", [128, 1], F32))
        one_c = E(nc.sbuf_tensor("one_c", [128, 1], F32))
        mshift_c = E(nc.sbuf_tensor("mshift_c", [128, 1], F32))
        mrs_c = E(nc.sbuf_tensor("mrs_c", [128, 1], F32))
        eps_c = E(nc.sbuf_tensor("eps_c", [128, 1], F32))
        nhalf_c = E(nc.sbuf_tensor("nhalf_c", [128, 1], F32))
        logit_sb = E(nc.sbuf_tensor("logit_sb", [128, 16], F32))

        sems = {}
        for name in ["dW", "dX", "dSp", "dO", "tS", "vS", "sS", "gS", "cc", "vSelf", "sSelf"]:
            sems[name] = Ctr(E(nc.semaphore(name)))
        dW, dX, dSp, dO = sems["dW"], sems["dX"], sems["dSp"], sems["dO"]
        tS, vS, sS, gS, cc = sems["tS"], sems["vS"], sems["sS"], sems["gS"], sems["cc"]
        vSelf, sSelf = sems["vSelf"], sems["sSelf"]

        block = E(nc.Block())


        def finish_early(written):
            zb = ctx.enter_context(nc.sbuf_tensor("zb_fill", [128, T_K], F32))
            @block.gpsimd
            def _(gp):
                gp.memset(zb[:], 0.0).then_inc(gS.sem, 1)
                gS.n += 1
            zf = gS.n

            @block.sync
            def _(sync):
                sync.wait_ge(gS.sem, zf)
                for gt in range(16):
                    sync.dma_start(out_d[gt * 128:(gt + 1) * 128, :], zb[:, 0:D]).then_inc(dO.sem, 16)
                    dO.n += 16
                for name, ap in dbg.items():
                    if name in written:
                        continue
                    pp, ff = ap.tensor.shape if hasattr(ap, "tensor") else (128, D)
                    sync.dma_start(ap, zb[0:pp, 0:ff]).then_inc(dO.sem, 16)
                    dO.n += 16
                sync.wait_ge(dO.sem, dO.n)


        def dma_seq(eng, ctr, out, in_):
            if ctr.n > 0:
                eng.wait_ge(ctr.sem, ctr.n)
            eng.dma_start(out, in_).then_inc(ctr.sem, 16)
            ctr.n += 16

        def mark(name, ctr):
            mil[name] = (ctr, ctr.n)

        waited = {}

        def eng_key(eng):
            if isinstance(eng, TEng):
                return "PE"
            if isinstance(eng, Eng):
                return "V" if eng.ctr.sem is sems["vSelf"].sem else "S"
            en = getattr(eng, "engine", None)
            return str(en)

        def _wait(eng, sem_obj, val):
            k = (eng_key(eng), id(sem_obj))
            if waited.get(k, -1) >= val:
                return
            waited[k] = val
            eng.wait_ge(sem_obj, val)

        def waitm(eng, name):
            ctr, val = mil[name]
            if val > 0:
                _wait(eng, ctr.sem, val)

        def guard_set(res, ctr):
            guards[res] = (ctr, ctr.n)

        def guard_wait(eng, res):
            if res in guards:
                ctr, val = guards[res]
                if val > 0:
                    _wait(eng, ctr.sem, val)

        @block.sync
        def _(sync):
            for dst, src in [
                (ident_s, ident_d), (psw_s, psw_d), (tri_s, tri_d), (w0p_s, w0p_d),
                (triu_s, triu_d), (iota_s, iota_d), (bm_s, bm_d), (stepc_s, stepc_d),
                (stepr_s, stepr_d), (repl_s, repl_d), (ones8_s, ones8_d),
                (ones128_s, ones128_d), (onesr_s, onesr_d), (c2q_s, c2q_d),
                (s2q_s, s2q_d), (c2k_s, c2k_d), (s2k_s, s2k_d),
                (br_s, br_d), (wr_f, wr_d),
            ]:
                dma_seq(sync, dX, dst[:], src)
            b1v = b1_d.rearrange("a (p c) -> a p c", c=1)
            b2v = b2_d.rearrange("a (p c) -> a p c", c=1)
            for i in range(DFF // 128):
                dma_seq(sync, dX, b1_s[i][:], b1v[i])
            for i in range(D // 128):
                dma_seq(sync, dX, b2_s[i][:], b2v[i])
        mark("consts", dX)

        @block.gpsimd
        def _(gp):
            gp.memset(zero_c[:], 0.0)
            gp.memset(one_c[:], 1.0)
            gp.memset(mshift_c[:], -SHIFT)
            gp.memset(mrs_c[:], -RSHIFT)
            gp.memset(eps_c[:], 1e-5)
            gp.memset(nhalf_c[:], -0.5).then_inc(gS.sem, 1)
            gS.n += 1
        mark("memset", gS)

        with ExitStack() as c1:
            E1 = c1.enter_context
            xT_sb = E1(nc.sbuf_tensor("xT_sb", [128, 16 * T_K], BF16))
            wq_sb = [E1(nc.sbuf_tensor(f"wq_sb{i}", [128, 16 * 256], BF16)) for i in range(2)]
            wk_sb = [E1(nc.sbuf_tensor(f"wk_sb{i}", [128, 16 * 256], BF16)) for i in range(2)]
            wv_sb = [E1(nc.sbuf_tensor(f"wv_sb{i}", [128, 16 * 256], BF16)) for i in range(2)]
            qraw_sb = E1(nc.sbuf_tensor("qraw_sb", [128, T_K], BF16))
            qsw_sb = E1(nc.sbuf_tensor("qsw_sb", [128, T_K], BF16))
            qh_sb = [E1(nc.sbuf_tensor(f"qh_sb{i}", [128, T_CORE], BF16)) for i in range(2)]
            kh_sb = [E1(nc.sbuf_tensor(f"kh_sb{i}", [128, T_K], BF16)) for i in range(2)]
            v_sb = E1(nc.sbuf_tensor("v_sb", [128, 18 * 256], BF16))
            probs_sb = E1(nc.sbuf_tensor("probs_sb", [128, 1024], BF16))
            probsF_sb = E1(nc.sbuf_tensor("probsF_sb", [128, 1024], F32))
            probsN_sb = E1(nc.sbuf_tensor("probsN_sb", [128, 1024], F32))
            recipA_t = E1(nc.sbuf_tensor("recipA_t", [128, 1], F32))
            recipB_t = E1(nc.sbuf_tensor("recipB_t", [128, 1], F32))
            probsT_sb = E1(nc.sbuf_tensor("probsT_sb", [128, 1024], BF16))
            sums_sb = E1(nc.sbuf_tensor("sums_sb", [128, 2], F32))
            sumA_t = E1(nc.sbuf_tensor("sumA_t", [128, 1], F32))
            sumB_t = E1(nc.sbuf_tensor("sumB_t", [128, 1], F32))
            recip_sb = E1(nc.sbuf_tensor("recip_sb", [128, 2], F32))
            ao_sb = [E1(nc.sbuf_tensor(f"ao_sb{i}", [128, T_CORE], BF16)) for i in range(2)]

            ps_pj = [E1(nc.psum_tensor(f"ps_pj{i}", [128, 512], F32)) for i in range(2)]
            ps_sw = [E1(nc.psum_tensor(f"ps_sw{i}", [128, 512], F32)) for i in range(2)]
            ps_scA = E1(nc.psum_tensor("ps_scA", [128, 512], F32))
            ps_scB = E1(nc.psum_tensor("ps_scB", [128, 512], F32))
            ps_pt = E1(nc.psum_tensor("ps_pt", [128, 1024], BF16))
            ps_av = E1(nc.psum_tensor("ps_av", [128, 256], F32))

            @block.gpsimd
            def _(gp):
                dma_seq(gp, dW, xT_sb[:].rearrange("p (a m) -> p a m", a=16),
                        xT_d.rearrange("(a p) m -> p a m", p=128))
            mark("xT", dW)

            def emit_wload(g):
                @block.gpsimd
                def _(gp, g=g):
                    guard_wait(gp, f"wslot{g % 2}")
                    for w_dd, w_ss in ((wq_d, wq_sb), (wk_d, wk_sb), (wv_d, wv_sb)):
                        dma_seq(gp, dW,
                                w_ss[g % 2][:].rearrange("p (a m) -> p a m", a=16),
                                w_dd[:, g * 256:(g + 1) * 256].rearrange("(a p) m -> p a m", p=128))
                mark(f"wg_{g}", dW)

            def emit_proj_rope(g, hh, which):
                h = 2 * g + hh
                sl = g % 2
                if which == "q":
                    w_sb_, tab_c, tab_s, out_sb, out_res, ncols, col0 = \
                        wq_sb, c2q_s, s2q_s, qh_sb[hh], f"qh{hh}", T_CORE, 256
                else:
                    w_sb_, tab_c, tab_s, out_sb, out_res, ncols, col0 = \
                        wk_sb, c2k_s, s2k_s, kh_sb[hh], f"kh{hh}", T_K, 0
                nch = (ncols + 511) // 512

                def pj_stage(c):
                    n = min(512, ncols - c * 512)

                    @block.tensor
                    def _(t, c=c, n=n):
                        t = TEng(t)
                        if c == 0:
                            waitm(t, f"wg_{g}")
                            waitm(t, "xT")
                            guard_wait(t, "qraw")
                        guard_wait(t, f"pj{c % 2}")
                        for i in range(16):
                            t.matmul(ps_pj[c % 2][:, 0:n],
                                     w_sb_[sl][:, i * 256 + hh * 128: i * 256 + hh * 128 + 128],
                                     xT_sb[:, i * T_K + col0 + c * 512: i * T_K + col0 + c * 512 + n],
                                     start=(i == 0), stop=(i == 15))
                        t.fin(tS.sem, 1)
                        tS.n += 1
                        mark(f"pj_{which}_{h}_{c}", tS)

                    @block.vector
                    def _(v, c=c, n=n):
                        v = Eng(v, vSelf)
                        waitm(v, f"pj_{which}_{h}_{c}")
                        v.tensor_copy(qraw_sb[:, c * 512: c * 512 + n],
                                      ps_pj[c % 2][:, 0:n]).then_inc(vS.sem, 1)
                        vS.n += 1
                        mark(f"evraw_{which}_{h}_{c}", vS)
                        guard_set(f"pj{c % 2}", vS)

                def sw_stage(c):
                    n = min(512, ncols - c * 512)

                    @block.tensor
                    def _(t, c=c, n=n):
                        t = TEng(t)
                        if c == 0:
                            guard_wait(t, "qsw")
                        waitm(t, f"evraw_{which}_{h}_{c}")
                        guard_wait(t, f"sw{c % 2}")
                        t.matmul(ps_sw[c % 2][:, 0:n], psw_s[:], qraw_sb[:, c * 512: c * 512 + n],
                                 start=True, stop=True)
                        t.fin(tS.sem, 1)
                        tS.n += 1
                        mark(f"sw_{which}_{h}_{c}", tS)

                    @block.vector
                    def _(v, c=c, n=n):
                        v = Eng(v, vSelf)
                        waitm(v, f"sw_{which}_{h}_{c}")
                        v.tensor_copy(qsw_sb[:, c * 512: c * 512 + n],
                                      ps_sw[c % 2][:, 0:n]).then_inc(vS.sem, 1)
                        vS.n += 1
                        mark(f"evsw_{which}_{h}_{c}", vS)
                        guard_set(f"sw{c % 2}", vS)

                pj_stage(0)
                for c in range(1, nch):
                    pj_stage(c)
                    sw_stage(c - 1)
                sw_stage(nch - 1)

                @block.vector
                def _(v):
                    v = Eng(v, vSelf)
                    waitm(v, f"evraw_{which}_{h}_{nch - 1}")
                    waitm(v, f"evsw_{which}_{h}_{nch - 1}")
                    guard_wait(v, out_res)
                    v.tensor_mul(qraw_sb[:, 0:ncols], qraw_sb[:, 0:ncols], tab_c[:, 0:ncols])
                    v.tensor_mul(qsw_sb[:, 0:ncols], qsw_sb[:, 0:ncols], tab_s[:, 0:ncols])
                    v.bar()
                    v.tensor_add(out_sb[:, 0:ncols], qraw_sb[:, 0:ncols], qsw_sb[:, 0:ncols]).then_inc(vS.sem, 1)
                    vS.n += 1
                    mark(f"rope_{which}_{h}", vS)
                    guard_set("qraw", vS)
                    guard_set("qsw", vS)

            def emit_vproj(g):
                sl = g % 2
                for jt in range(18):
                    @block.tensor
                    def _(t, jt=jt):
                        t = TEng(t)
                        if jt == 0:
                            waitm(t, f"wg_{g}")
                            waitm(t, "xT")
                        guard_wait(t, f"pj{jt % 2}")
                        for i in range(16):
                            t.matmul(ps_pj[jt % 2][:, 0:256],
                                     xT_sb[:, i * T_K + jt * 128: i * T_K + jt * 128 + 128],
                                     wv_sb[sl][:, i * 256: i * 256 + 256],
                                     start=(i == 0), stop=(i == 15))
                        t.fin(tS.sem, 1)
                        tS.n += 1
                        mark(f"vpj_{g}_{jt}", tS)

                    @block.vector
                    def _(v, jt=jt):
                        v = Eng(v, vSelf)
                        if jt == 0:
                            guard_wait(v, "v_sb")
                        waitm(v, f"vpj_{g}_{jt}")
                        v.tensor_copy(v_sb[:, jt * 256: jt * 256 + 256],
                                      ps_pj[jt % 2][:, 0:256]).then_inc(vS.sem, 1)
                        vS.n += 1
                        mark(f"evv_{g}_{jt}", vS)
                        guard_set(f"pj{jt % 2}", vS)

            def emit_attention(g, hh):
                h = 2 * g + hh
                qh, kh = qh_sb[hh], kh_sb[hh]

                def sc_stage(w):
                    @block.tensor
                    def _(t, w=w):
                        t = TEng(t)
                        if w == 0:
                            waitm(t, f"rope_q_{h}")
                            waitm(t, f"rope_k_{h}")
                            waitm(t, f"evv_{g}_17")
                            waitm(t, "consts")
                        guard_wait(t, "scA")
                        guard_wait(t, "scB")
                        t.matmul(ps_scA[:, 256:384], ident_s[:], tri_s[:], start=True, stop=True)
                        t.matmul(ps_scB[:, 384:512], ident_s[:], tri_s[:], start=True, stop=True)
                        if w == 0:
                            t.matmul(ps_scA[:, 0:256], ident_s[:], w0p_s[:], start=False, stop=True)
                            t.matmul(ps_scB[:, 0:256], ident_s[:], w0p_s[:], start=False, stop=True)
                        t.matmul(ps_scA[:, 0:384], qh[:, w * 256: w * 256 + 128],
                                 kh[:, w * 256: w * 256 + 384], start=False, stop=True)
                        t.matmul(ps_scB[:, 0:512], qh[:, w * 256 + 128: w * 256 + 256],
                                 kh[:, w * 256: w * 256 + 512], start=False, stop=True)
                        t.fin(tS.sem, 1)
                        tS.n += 1
                        mark(f"score_{h}_{w}", tS)

                    @block.scalar
                    def _(s, w=w):
                        s = Eng(s, sSelf)
                        waitm(s, f"score_{h}_{w}")
                        guard_wait(s, "probsF")
                        s.activation(probsF_sb[:, 0:384], ps_scA[:, 0:384], AF.Exp,
                                     bias=mshift_c[:], scale=one_c[:])
                        s.activation(probsF_sb[:, 512:1024], ps_scB[:, 0:512], AF.Exp,
                                     bias=mshift_c[:], scale=one_c[:]).then_inc(sS.sem, 1)
                        sS.n += 1
                        mark(f"exp_{h}_{w}", sS)
                        guard_set("scA", sS)
                        guard_set("scB", sS)

                    @block.vector
                    def _(v, w=w):
                        v = Eng(v, vSelf)
                        waitm(v, f"exp_{h}_{w}")
                        v.reduce_sum(sumA_t[:], probsF_sb[:, 0:384], axis=AX.X)
                        v.reduce_sum(sumB_t[:], probsF_sb[:, 512:1024], axis=AX.X)
                        v.bar()
                        v.tensor_scalar(sumA_t[:], sumA_t[:], 1e-30, None, op0=ALU.max)
                        v.tensor_scalar(sumB_t[:], sumB_t[:], 1e-30, None, op0=ALU.max)
                        v.bar()
                        v.reciprocal(recipA_t[:], sumA_t[:])
                        v.reciprocal(recipB_t[:], sumB_t[:])
                        v.bar()
                        guard_wait(v, "probs")
                        v.tensor_scalar(probsN_sb[:, 0:384], probsF_sb[:, 0:384],
                                        recipA_t[:], None, op0=ALU.mult)
                        v.tensor_scalar(probsN_sb[:, 512:1024], probsF_sb[:, 512:1024],
                                        recipB_t[:], None, op0=ALU.mult)
                        v.bar()
                        v.tensor_copy(probs_sb[:, 0:384], probsN_sb[:, 0:384])
                        v.tensor_copy(probs_sb[:, 512:1024], probsN_sb[:, 512:1024]).then_inc(vS.sem, 1)
                        vS.n += 1
                        mark(f"norm_{h}_{w}", vS)
                        guard_set("probsF", vS)

                def av_stage(w):
                    @block.tensor
                    def _(t, w=w):
                        t = TEng(t)
                        waitm(t, f"norm_{h}_{w}")
                        guard_wait(t, "ptv")
                        for j in range(3):
                            t.transpose(ps_pt[:, j * 256: j * 256 + 128],
                                        probs_sb[:, j * 128: j * 128 + 128], ident_s[:])
                        for j in range(4):
                            t.transpose(ps_pt[:, j * 256 + 128: j * 256 + 256],
                                        probs_sb[:, 512 + j * 128: 512 + j * 128 + 128], ident_s[:])
                        t.fin(tS.sem, 1)
                        tS.n += 1
                        mark(f"trp_{h}_{w}", tS)
                        guard_set("probs", tS)

                    @block.vector
                    def _(v, w=w):
                        v = Eng(v, vSelf)
                        waitm(v, f"trp_{h}_{w}")
                        guard_wait(v, "probsT")
                        for j in range(3):
                            v.tensor_copy(probsT_sb[:, j * 256: j * 256 + 128],
                                          ps_pt[:, j * 256: j * 256 + 128])
                            v.tensor_copy(probsT_sb[:, j * 256 + 128: j * 256 + 256],
                                          ps_pt[:, j * 256 + 128: j * 256 + 256])
                        v.tensor_copy(probsT_sb[:, 3 * 256 + 128: 4 * 256],
                                      ps_pt[:, 3 * 256 + 128: 4 * 256]).then_inc(vS.sem, 1)
                        vS.n += 1
                        mark(f"ptv_{h}_{w}", vS)

                    @block.tensor
                    def _(t, w=w):
                        t = TEng(t)
                        waitm(t, f"ptv_{h}_{w}")
                        guard_wait(t, "av")
                        for j in range(4):
                            kt = 2 * w + j
                            if j < 3:
                                t.matmul(ps_av[:, 0:256],
                                         v_sb[:, kt * 256 + hh * 128: kt * 256 + hh * 128 + 128],
                                         probsT_sb[:, j * 256: j * 256 + 256],
                                         start=(j == 0), stop=False)
                            else:
                                t.matmul(ps_av[:, 128:256],
                                         v_sb[:, kt * 256 + hh * 128: kt * 256 + hh * 128 + 128],
                                         probsT_sb[:, j * 256 + 128: j * 256 + 256],
                                         start=False, stop=True)
                        t.fin(tS.sem, 1)
                        tS.n += 1
                        mark(f"av_{h}_{w}", tS)
                        guard_set("probsT", tS)

                    @block.vector
                    def _(v, w=w):
                        v = Eng(v, vSelf)
                        waitm(v, f"av_{h}_{w}")
                        if w == 0:
                            guard_wait(v, f"ao{hh}")
                        v.tensor_copy(ao_sb[hh][:, w * 256: w * 256 + 256],
                                      ps_av[:, 0:256]).then_inc(vS.sem, 1)
                        vS.n += 1
                        mark(f"aoev_{h}_{w}", vS)
                        guard_set("av", vS)

                sc_stage(0)
                for w in range(1, NW):
                    sc_stage(w)
                    av_stage(w - 1)
                av_stage(NW - 1)

                guard_set(f"qh{hh}", tS)
                guard_set(f"kh{hh}", tS)

                @block.sync
                def _(sync):
                    waitm(sync, f"aoev_{h}_{NW - 1}")
                    sync.dma_start(ao_hbm.ap()[h * 128:(h + 1) * 128, :], ao_sb[hh][:]).then_inc(dSp.sem, 16)
                    dSp.n += 16
                mark(f"aospill_{h}", dSp)
                guard_set(f"ao{hh}", dSp)

            emit_wload(0)
            if ngroups > 1:
                emit_wload(1)
            for g in range(ngroups):
                if p1_upto >= 1:
                    emit_vproj(g)
                for hh in range(2):
                    if p1_upto >= 2:
                        emit_proj_rope(g, hh, "q")
                    if p1_upto >= 3:
                        emit_proj_rope(g, hh, "k")
                    if p1_upto >= 4:
                        emit_attention(g, hh)
                guard_set("v_sb", tS)
                guard_set(f"wslot{g % 2}", tS)
                if g + 2 < ngroups:
                    emit_wload(g + 2)

            if debug:
                @block.gpsimd
                def _(gp):
                    if "rope_q_0" in mil:
                        waitm(gp, "rope_q_0")
                        gp.dma_start(dbg["dbg_q0"], qh_sb[0][:]).then_inc(dX.sem, 16)
                        dX.n += 16
                        dbg_written.add("dbg_q0")
                    if "rope_k_0" in mil:
                        waitm(gp, "rope_k_0")
                        gp.dma_start(dbg["dbg_k0"], kh_sb[0][:]).then_inc(dX.sem, 16)
                        dX.n += 16
                        dbg_written.add("dbg_k0")
                    if "evv_0_17" in mil:
                        waitm(gp, "evv_0_17")
                        gp.dma_start(dbg["dbg_v0"], v_sb[:, 0:256]).then_inc(dX.sem, 16)
                        dX.n += 16
                        dbg_written.add("dbg_v0")
                    if "aoev_0_7" in mil:
                        waitm(gp, f"aospill_{2 * ngroups - 1}")
                        gp.dma_start(dbg["dbg_ao0"], ao_hbm.ap()[0:128, :]).then_inc(dX.sem, 16)
                        dX.n += 16
                        dbg_written.add("dbg_ao0")

        if phases < 2:
            endk = f"aospill_{2 * ngroups - 1}"
            if endk not in mil:
                endk = None
            @block.sync
            def _(sync):
                if endk is not None:
                    sync.wait_ge(dSp.sem, mil[endk][1])
                else:
                    sync.wait_ge(sS.sem, sS.n)
                    sync.wait_ge(vS.sem, vS.n)
                    sync.wait_ge(tS.sem, tS.n)
            finish_early(dbg_written)
            return nc, dbg

        with ExitStack() as c2:
            E2 = c2.enter_context
            wo_sb = E2(nc.sbuf_tensor("wo_sb", [128, 16 * D], BF16))
            at_sb = [E2(nc.sbuf_tensor(f"at_sb{i}", [128, 16 * 256], BF16)) for i in range(2)]
            xc_sb = [E2(nc.sbuf_tensor(f"xc_sb{i}", [128, 2 * D], F32)) for i in range(2)]
            x1s_sb = [E2(nc.sbuf_tensor(f"x1s_sb{i}", [128, D], F32)) for i in range(2)]
            sqa_sb = E2(nc.sbuf_tensor("sqa_sb", [128, D], F32))
            sqb_sb = E2(nc.sbuf_tensor("sqb_sb", [128, D], F32))
            st_sb = E2(nc.sbuf_tensor("st_sb", [128, 12], F32))
            sumsq_t = E2(nc.sbuf_tensor("sumsq_t", [128, 1], F32))
            rstd_t = E2(nc.sbuf_tensor("rstd_t", [128, 1], F32))
            m_t = E2(nc.sbuf_tensor("m_t", [128, 1], F32))
            ps_y = [E2(nc.psum_tensor(f"ps_y{i}", [128, 2048], F32)) for i in range(2)]

            @block.gpsimd
            def _(gp):
                gp.wait_ge(dSp.sem, mil["aospill_15"][1])
                for half in range(2):
                    dma_seq(gp, dW, wo_sb[:, half * 8 * D:(half + 1) * 8 * D]
                            .rearrange("p (a m) -> p a m", a=8),
                            wo_d[half * 1024:(half + 1) * 1024, :]
                            .rearrange("(a p) m -> p a m", p=128))
                    mark(f"wo_{half}", dW)

            for cch in range(8):
                @block.sync
                def _(sync, cch=cch):
                    guard_wait(sync, f"p2slot{cch % 2}")
                    sync.wait_ge(dSp.sem, mil["aospill_15"][1])
                    dma_seq(sync, dX, at_sb[cch % 2][:].rearrange("p (a m) -> p a m", a=16),
                            ao_hbm.ap()[:, cch * 256:(cch + 1) * 256]
                            .rearrange("(a p) m -> p a m", p=128))
                    dma_seq(sync, dX, xc_sb[cch % 2][:].rearrange("p (a m) -> p a m", a=2),
                            x_d[cch * 256:(cch + 1) * 256, :]
                            .rearrange("(a p) m -> p a m", p=128))
                mark(f"p2load_{cch}", dX)

                for tt in range(2):
                    gt = cch * 2 + tt
                    py = ps_y[tt]

                    @block.tensor
                    def _(t, cch=cch, tt=tt, py=py):
                        t = TEng(t)
                        waitm(t, f"p2load_{cch}")
                        guard_wait(t, f"py{tt}")
                        for i_half in range(2):
                            waitm(t, f"wo_{i_half}")
                            for i in range(i_half * 8, i_half * 8 + 8):
                                for oc in range(4):
                                    t.matmul(py[:, oc * 512: oc * 512 + 512],
                                             at_sb[cch % 2][:, i * 256 + tt * 128: i * 256 + tt * 128 + 128],
                                             wo_sb[:, i * D + oc * 512: i * D + oc * 512 + 512],
                                             start=(i == 0), stop=(i == 15))
                        t.fin(tS.sem, 1)
                        tS.n += 1
                        mark(f"y_{gt}", tS)

                    @block.vector
                    def _(v, cch=cch, tt=tt, py=py, gt=gt):
                        v = Eng(v, vSelf)
                        waitm(v, f"y_{gt}")
                        guard_wait(v, f"x1s{gt % 2}")
                        for oc in range(4):
                            v.tensor_add(x1s_sb[gt % 2][:, oc * 512: oc * 512 + 512],
                                         py[:, oc * 512: oc * 512 + 512],
                                         xc_sb[cch % 2][:, tt * D + oc * 512: tt * D + oc * 512 + 512])
                        v.bar()
                        v.reduce_sum(st_sb[:, 0:1], x1s_sb[gt % 2][:], axis=AX.X).then_inc(vS.sem, 1)
                        vS.n += 1
                        mark(f"x1pre_{gt}", vS)
                        guard_set(f"py{tt}", vS)
                        if tt == 1:
                            guard_set(f"p2slot{cch % 2}", vS)

                    @block.scalar
                    def _(s, gt=gt):
                        s = Eng(s, sSelf)
                        waitm(s, f"x1pre_{gt}")
                        guard_wait(s, "sqa")
                        s.activation(sqa_sb[:], x1s_sb[gt % 2][:], AF.Square,
                                     accum_out=sumsq_t[:]).then_inc(sS.sem, 1)
                        sS.n += 1
                        mark(f"sumsq_{gt}", sS)

                    @block.vector
                    def _(v, gt=gt):
                        v = Eng(v, vSelf)
                        waitm(v, f"sumsq_{gt}")
                        v.tensor_scalar(m_t[:], st_sb[:, 0:1], 1.0 / D, None, op0=ALU.mult)
                        v.tensor_scalar(st_sb[:, 3:4], sumsq_t[:], 1.0 / D, None, op0=ALU.mult)
                        v.bar()
                        v.tensor_mul(st_sb[:, 4:5], m_t[:], m_t[:])
                        v.bar()
                        v.tensor_sub(st_sb[:, 5:6], st_sb[:, 3:4], st_sb[:, 4:5])
                        v.bar()
                        v.tensor_scalar(st_sb[:, 5:6], st_sb[:, 5:6], 0.0, None, op0=ALU.max).then_inc(vS.sem, 1)
                        vS.n += 1
                        mark(f"var_{gt}", vS)
                        guard_set("sqa", vS)

                    @block.scalar
                    def _(s, gt=gt):
                        s = Eng(s, sSelf)
                        waitm(s, f"var_{gt}")
                        s.activation(st_sb[:, 6:7], st_sb[:, 5:6], AF.Ln, bias=eps_c[:], scale=one_c[:])
                        s.bar()
                        s.activation(rstd_t[:], st_sb[:, 6:7], AF.Exp, bias=zero_c[:],
                                     scale=nhalf_c[:]).then_inc(sS.sem, 1)
                        sS.n += 1
                        mark(f"rstd_{gt}", sS)

                    @block.vector
                    def _(v, gt=gt):
                        v = Eng(v, vSelf)
                        waitm(v, f"rstd_{gt}")
                        v.tensor_scalar(x1s_sb[gt % 2][:], x1s_sb[gt % 2][:], m_t[:], rstd_t[:],
                                        op0=ALU.subtract, op1=ALU.mult).then_inc(vS.sem, 1)
                        vS.n += 1
                        mark(f"x1_{gt}", vS)

                    @block.vector
                    def _(v, gt=gt):
                        v = Eng(v, vSelf)
                        waitm(v, f"x1_{gt}")
                        guard_wait(v, "sqb")
                        v.tensor_mul(sqb_sb[:], x1s_sb[gt % 2][:], wr_f[:])
                        v.bar()
                        v.reduce_sum(logit_sb[:, gt:gt + 1], sqb_sb[:], axis=AX.X)
                        v.bar()
                        v.tensor_add(logit_sb[:, gt:gt + 1], logit_sb[:, gt:gt + 1],
                                     br_s[:]).then_inc(vS.sem, 1)
                        vS.n += 1
                        mark(f"lgt_{gt}", vS)
                        guard_set("sqb", vS)

                    @block.sync
                    def _(sync, gt=gt):
                        waitm(sync, f"lgt_{gt}")
                        sync.dma_start(x1_hbm.ap()[gt * 128:(gt + 1) * 128, :],
                                       x1s_sb[gt % 2][:]).then_inc(dSp.sem, 16)
                        dSp.n += 16
                    mark(f"x1spill_{gt}", dSp)
                    guard_set(f"x1s{gt % 2}", dSp)

            if debug:
                @block.gpsimd
                def _(gp):
                    waitm(gp, "x1spill_15")
                    gp.dma_start(dbg["dbg_x1a"], x1s_sb[1][:]).then_inc(dX.sem, 16)
                    gp.dma_start(dbg["dbg_logit"], logit_sb[:]).then_inc(dX.sem, 16)
                    dX.n += 32
                    dbg_written.update(["dbg_x1a", "dbg_logit"])

        if phases < 3:
            @block.sync
            def _(sync):
                sync.wait_ge(dSp.sem, mil["x1spill_15"][1])
            finish_early(dbg_written)
            return nc, dbg

        c34 = ctx.enter_context(ExitStack())
        PT_sb = c34.enter_context(nc.sbuf_tensor("PT_sb", [128, 16 * CAP], BF16))
        Pw_sb = c34.enter_context(nc.sbuf_tensor("Pw_sb", [128, 48 * 128], BF16))

        with ExitStack() as c3:
            E3 = c3.enter_context
            g8_sb = E3(nc.sbuf_tensor("g8_sb", [8, 1024], F32))
            grep_sb = E3(nc.sbuf_tensor("grep_sb", [128, 1024], F32))
            cscr_sb = E3(nc.sbuf_tensor("cscr_sb", [128, 1024], F32))
            e8_sb = E3(nc.sbuf_tensor("e8_sb", [8, 1024], F32))
            m8_sb = E3(nc.sbuf_tensor("m8_sb", [8, 1024], F32))
            sr8_sb = E3(nc.sbuf_tensor("sr8_sb", [8, 1], F32))
            cnt_sb = E3(nc.sbuf_tensor("cnt_sb", [128, 1], F32))
            tmp16_sb = E3(nc.sbuf_tensor("tmp16_sb", [128, NPROBE], F32))
            crow_sb = E3(nc.sbuf_tensor("crow_sb", [1, NPROBE], F32))
            brow_sb = E3(nc.sbuf_tensor("brow_sb", [1, 8 * NPROBE], F32))
            lh_sb = E3(nc.sbuf_tensor("lh_sb", [1, 8], F32))
            tau_sb = E3(nc.sbuf_tensor("tau_sb", [128, 1], F32))
            loc_sb = E3(nc.sbuf_tensor("loc_sb", [128, 1], F32))
            spanc_sb = E3(nc.sbuf_tensor("spanc_sb", [128, 1], F32))
            dn_sb = E3(nc.sbuf_tensor("dn_sb", [1, 2], F32))
            recd_sb = E3(nc.sbuf_tensor("recd_sb", [128, 1], F32))
            mask16_sb = E3(nc.sbuf_tensor("mask16_sb", [128, 16], F32))
            maskb_sb = E3(nc.sbuf_tensor("maskb_sb", [128, 16], BF16))
            e16_sb = E3(nc.sbuf_tensor("e16_sb", [128, 16], F32))
            rw16_sb = E3(nc.sbuf_tensor("rw16_sb", [128, 16], F32))
            q16_sb = E3(nc.sbuf_tensor("q16_sb", [128, 16], F32))
            pcol_sb = E3(nc.sbuf_tensor("pcol_sb", [1, 16], F32))
            pbc_sb = E3(nc.sbuf_tensor("pbc_sb", [128, 16], F32))
            pt_f32 = E3(nc.sbuf_tensor("pt_f32", [128, CAP], F32))
            pw_f32 = E3(nc.sbuf_tensor("pw_f32", [128, CAP], F32))
            PwT_sb = E3(nc.sbuf_tensor("PwT_sb", [128, CAP], BF16))
            qcol_t = E3(nc.sbuf_tensor("qcol_t", [128, 1], F32))
            rwcol_t = E3(nc.sbuf_tensor("rwcol_t", [128, 1], F32))
            lo_t = E3(nc.sbuf_tensor("lo_t", [1, 1], F32))
            hi_t = E3(nc.sbuf_tensor("hi_t", [1, 1], F32))
            span_t = E3(nc.sbuf_tensor("span_t", [1, 1], F32))
            red_t = E3(nc.sbuf_tensor("red_t", [1, 1], F32))
            red2_t = E3(nc.sbuf_tensor("red2_t", [1, 1], F32))

            dgR_sb = E3(nc.sbuf_tensor("dgR_sb", [128, 8], F32))
            dgG_sb = E3(nc.sbuf_tensor("dgG_sb", [128, 1024], F32))
            ps_r1 = E3(nc.psum_tensor("ps_r1", [128, 1024], F32))
            ps_r2 = E3(nc.psum_tensor("ps_r2", [1, 16], F32))
            ps_r3 = E3(nc.psum_tensor("ps_r3", [128, 16], F32))
            ps_r4 = E3(nc.psum_tensor("ps_r4", [1, 2], F32))
            ps_r5 = E3(nc.psum_tensor("ps_r5", [128, CAP], BF16))

            @block.gpsimd
            def _(gp):
                waitm(gp, "lgt_15")
                waitm(gp, "x1spill_15")
                gp.dma_start(agin.ap().rearrange("a (p t) -> p (a t)", p=128),
                             logit_sb[:]).then_inc(dSp.sem, 16)
                dSp.n += 16
                gp.wait_ge(dSp.sem, dSp.n)
                gp.collective_compute(
                    "AllGather", ALU.bypass,
                    replica_groups=[[0, 1, 2, 3], [4, 5, 6, 7]],
                    ins=[agin.ap().opt()],
                    outs=[agout.ap().opt()],
                ).then_inc(cc.sem)
                cc.n += 1
                gp.wait_ge(cc.sem, cc.n)
                dma_seq(gp, dX, g8_sb[:], agout.ap().rearrange("a (c d) -> (a c) d", d=1024))
            mark("g8", dX)

            @block.tensor
            def _(t):
                t = TEng(t)
                waitm(t, "g8")
                t.matmul(ps_r1[:, 0:512], repl_s[:], g8_sb[:, 0:512], start=True, stop=True)
                t.matmul(ps_r1[:, 512:1024], repl_s[:], g8_sb[:, 512:1024], start=True, stop=True)
                t.fin(tS.sem, 1)
                tS.n += 1
            mark("grep_mm", tS)

            @block.vector
            def _(v):
                v = Eng(v, vSelf)
                waitm(v, "grep_mm")
                v.tensor_copy(grep_sb[:], ps_r1[:])

                v.memset(lo_t[:], -64.0)
                v.memset(hi_t[:], 64.0)
                v.memset(span_t[:], 128.0).then_inc(vS.sem, 1)
                vS.n += 1
            mark("grep", vS)

            for rnd in range(NROUND):
                @block.tensor
                def _(t, rnd=rnd):
                    t = TEng(t)
                    waitm(t, f"lohi_{rnd - 1}" if rnd > 0 else "grep")
                    t.matmul(ps_r3[:, 0:1], onesr_s[:], lo_t[:], start=True, stop=True)
                    t.matmul(ps_r3[:, 1:2], onesr_s[:], span_t[:], start=True, stop=True)
                    t.fin(tS.sem, 1)
                    tS.n += 1
                    mark(f"bc_{rnd}", tS)

                @block.vector
                def _(v, rnd=rnd):
                    v = Eng(v, vSelf)
                    waitm(v, f"bc_{rnd}")
                    v.tensor_copy(loc_sb[:], ps_r3[:, 0:1])
                    v.tensor_copy(spanc_sb[:], ps_r3[:, 1:2])
                    v.bar()
                    v.tensor_mul(tau_sb[:], spanc_sb[:], stepc_s[:])
                    v.bar()
                    v.tensor_add(tau_sb[:], tau_sb[:], loc_sb[:])
                    v.bar()
                    v.tensor_scalar(cscr_sb[:], grep_sb[:], tau_sb[:], 0.0, op0=ALU.is_gt, op1=ALU.add)
                    v.bar()
                    v.reduce_sum(cnt_sb[:], cscr_sb[:], axis=AX.X)
                    v.bar()
                    v.tensor_scalar(tmp16_sb[:], bm_s[:], cnt_sb[:], None, op0=ALU.mult).then_inc(vS.sem, 1)
                    vS.n += 1
                    mark(f"cnt_{rnd}", vS)


                @block.tensor
                def _(t, rnd=rnd):
                    t = TEng(t)
                    waitm(t, f"cnt_{rnd}")
                    t.matmul(ps_r2[:, :], ones128_s[:], tmp16_sb[:], start=True, stop=True)
                    t.fin(tS.sem, 1)
                    tS.n += 1
                    mark(f"cnt2_{rnd}", tS)

                @block.vector
                def _(v, rnd=rnd):
                    v = Eng(v, vSelf)
                    waitm(v, f"cnt2_{rnd}")
                    v.tensor_copy(crow_sb[:], ps_r2[:, :])
                    v.bar()
                    P = NPROBE
                    taur = brow_sb[0:1, 0:P]
                    ge = brow_sb[0:1, P:2 * P]
                    cl = brow_sb[0:1, 2 * P:3 * P]
                    red = brow_sb[0:1, 3 * P:3 * P + 1]
                    red2 = brow_sb[0:1, 3 * P + 1:3 * P + 2]
                    ch = brow_sb[0:1, 4 * P:5 * P]
                    v.tensor_scalar(taur, stepr_s[:], span_t[:], None, op0=ALU.mult)
                    v.bar()
                    v.tensor_scalar(taur, taur, lo_t[:], None, op0=ALU.add)
                    v.bar()
                    v.tensor_scalar(ge, crow_sb[:], float(KTOP), None, op0=ALU.is_ge)
                    v.bar()
                    t2 = brow_sb[0:1, 5 * P:6 * P]
                    v.tensor_mul(cl, taur, ge)
                    v.tensor_scalar(t2, ge, 1e30, -1e30, op0=ALU.mult, op1=ALU.add)
                    v.bar()
                    v.tensor_add(cl, cl, t2)
                    v.bar()
                    v.reduce_max(red_t[:], cl, axis=AX.X)
                    v.bar()
                    v.tensor_max(lo_t[:], lo_t[:], red_t[:])
                    v.bar()
                    v.tensor_scalar(ge, ge, -1.0, 1.0, op0=ALU.mult, op1=ALU.add)
                    v.bar()
                    v.tensor_mul(ch, taur, ge)
                    v.tensor_scalar(t2, ge, -1e30, 1e30, op0=ALU.mult, op1=ALU.add)
                    v.bar()
                    v.tensor_add(ch, ch, t2)
                    v.bar()
                    v.tensor_reduce(red2_t[:], ch, axis=AX.X, op=ALU.min)
                    v.bar()
                    v.tensor_tensor(hi_t[:], hi_t[:], red2_t[:], op=ALU.min)
                    v.bar()
                    v.tensor_sub(span_t[:], hi_t[:], lo_t[:]).then_inc(vS.sem, 1)
                    vS.n += 1
                    mark(f"lohi_{rnd}", vS)
                    if debug == 2 and rnd == 0:
                        v.wait_ge(vS.sem, vS.n)
                        v.tensor_copy(dgG_sb[0:1, 0:128], brow_sb[0:1, 0:128])
                        v.tensor_copy(dgG_sb[0:1, 200:216], crow_sb[:])
                        v.tensor_copy(dgG_sb[0:1, 300:301], lo_t[:])
                        v.tensor_copy(dgG_sb[0:1, 301:302], hi_t[:])
                        v.tensor_copy(dgG_sb[0:1, 302:303], span_t[:])
                        v.tensor_copy(dgG_sb[0:1, 303:304], red_t[:])
                        v.tensor_copy(dgG_sb[0:1, 304:305], red2_t[:])
                        v.tensor_copy(dgG_sb[:, 400:401], tau_sb[:])
                        v.tensor_copy(dgG_sb[:, 401:402], cnt_sb[:]).then_inc(vS.sem, 1)
                        vS.n += 1
                        mark("dgR", vS)

            @block.tensor
            def _(t):
                t = TEng(t)
                waitm(t, f"lohi_{NROUND - 1}")
                t.matmul(ps_r3[:, 0:1], onesr_s[:], lo_t[:], start=True, stop=True)
                t.fin(tS.sem, 1)
                tS.n += 1
            mark("taubc", tS)

            @block.vector
            def _(v):
                v = Eng(v, vSelf)
                waitm(v, "taubc")
                v.tensor_copy(tau_sb[:], ps_r3[:, 0:1]).then_inc(vS.sem, 1)
                vS.n += 1
            mark("tau", vS)

            @block.scalar
            def _(s):
                s = Eng(s, sSelf)
                waitm(s, "g8")
                s.activation(e8_sb[:], g8_sb[:], AF.Exp, bias=mrs_c[0:8, :],
                             scale=one_c[0:8, :]).then_inc(sS.sem, 1)
                sS.n += 1
            mark("e8", sS)

            @block.vector
            def _(v):
                v = Eng(v, vSelf)
                waitm(v, "e8")
                waitm(v, "tau")
                v.tensor_scalar(m8_sb[:], g8_sb[:], tau_sb[0:8, :], None, op0=ALU.is_gt)
                v.bar()
                v.tensor_mul(e8_sb[:], e8_sb[:], m8_sb[:])
                v.bar()
                v.reduce_sum(sr8_sb[:], e8_sb[:], axis=AX.X).then_inc(vS.sem, 1)
                vS.n += 1
            mark("e8m", vS)

            @block.tensor
            def _(t):
                t = TEng(t)
                waitm(t, "e8m")
                t.matmul(ps_r4[:, 0:1], ones8_s[:], sr8_sb[:], start=True, stop=True)
                t.fin(tS.sem, 1)
                tS.n += 1
            mark("dn_mm", tS)

            @block.vector
            def _(v):
                v = Eng(v, vSelf)
                waitm(v, "dn_mm")
                v.tensor_copy(dn_sb[0:1, 0:1], ps_r4[:, 0:1])
                v.bar()
                v.tensor_scalar(dn_sb[0:1, 0:1], dn_sb[0:1, 0:1], 1e-30, None, op0=ALU.max)
                v.bar()
                v.reciprocal(red_t[:], dn_sb[0:1, 0:1]).then_inc(vS.sem, 1)
                vS.n += 1
            mark("dn", vS)

            @block.tensor
            def _(t):
                t = TEng(t)
                waitm(t, "dn")
                t.matmul(ps_r3[:, 2:3], onesr_s[:], red_t[:], start=True, stop=True)
                t.fin(tS.sem, 1)
                tS.n += 1
            mark("recd_mm", tS)

            @block.vector
            def _(v):
                v = Eng(v, vSelf)
                waitm(v, "recd_mm")
                v.tensor_copy(recd_sb[:], ps_r3[:, 2:3])
                v.tensor_scalar(mask16_sb[:], logit_sb[:], tau_sb[:], None, op0=ALU.is_gt)
                v.bar()
                v.tensor_copy(maskb_sb[:], mask16_sb[:]).then_inc(vS.sem, 1)
                vS.n += 1
            mark("mask16", vS)

            @block.scalar
            def _(s):
                s = Eng(s, sSelf)
                waitm(s, "mask16")
                s.activation(e16_sb[:], logit_sb[:], AF.Exp, bias=mrs_c[:],
                             scale=one_c[:]).then_inc(sS.sem, 1)
                sS.n += 1
            mark("e16", sS)

            @block.vector
            def _(v):
                v = Eng(v, vSelf)
                waitm(v, "e16")
                v.tensor_scalar(rw16_sb[:], e16_sb[:], recd_sb[:], None, op0=ALU.mult)
                v.bar()
                v.tensor_mul(rw16_sb[:], rw16_sb[:], mask16_sb[:]).then_inc(vS.sem, 1)
                vS.n += 1
            mark("rw16", vS)

            @block.tensor
            def _(t):
                t = TEng(t)
                waitm(t, "mask16")
                t.matmul(ps_r3[:, 0:16], triu_s[:], maskb_sb[:], start=True, stop=True)
                t.matmul(ps_r2[:, :], ones128_s[:], mask16_sb[:], start=True, stop=True)
                t.fin(tS.sem, 1)
                tS.n += 1
            mark("cums_mm", tS)

            @block.vector
            def _(v):
                v = Eng(v, vSelf)
                waitm(v, "cums_mm")
                v.tensor_copy(q16_sb[:], ps_r3[:, 0:16])
                v.tensor_copy(pcol_sb[:], ps_r2[:, :])
                v.bar()
                a0 = brow_sb[0:1, 0:16]
                b0 = brow_sb[0:1, 16:32]
                v.tensor_copy(a0, pcol_sb[:])
                v.bar()
                for sh in (1, 2, 4, 8):
                    v.tensor_copy(b0, a0)
                    v.bar()
                    v.tensor_add(b0[0:1, sh:16], a0[0:1, sh:16], a0[0:1, 0:16 - sh])
                    v.bar()
                    v.tensor_copy(a0, b0)
                    v.bar()
                v.tensor_sub(pcol_sb[:], a0, pcol_sb[:]).then_inc(vS.sem, 1)
                vS.n += 1
            mark("prefix", vS)

            @block.tensor
            def _(t):
                t = TEng(t)
                waitm(t, "prefix")
                t.matmul(ps_r3[:, 0:16], onesr_s[:], pcol_sb[:], start=True, stop=True)
                t.fin(tS.sem, 1)
                tS.n += 1
            mark("pfx_bc", tS)

            @block.vector
            def _(v):
                v = Eng(v, vSelf)
                waitm(v, "pfx_bc")
                v.tensor_copy(pbc_sb[:], ps_r3[:, 0:16])
                v.bar()
                v.tensor_add(q16_sb[:], q16_sb[:], pbc_sb[:])
                v.bar()
                v.tensor_mul(q16_sb[:], q16_sb[:], mask16_sb[:])
                v.tensor_copy(lh_sb[0:1, 0:1], lo_t[:])
                v.tensor_copy(lh_sb[0:1, 1:2], hi_t[:])
                v.tensor_copy(lh_sb[0:1, 2:3], span_t[:]).then_inc(vS.sem, 1)
                vS.n += 1
            mark("Q", vS)

            for i in range(16):
                @block.vector
                def _(v, i=i):
                    v = Eng(v, vSelf)
                    waitm(v, "Q")
                    waitm(v, "rw16")
                    guard_wait(v, "PwT")
                    v.tensor_copy(qcol_t[:], q16_sb[:, i:i + 1])
                    v.tensor_copy(rwcol_t[:], rw16_sb[:, i:i + 1])
                    v.bar()
                    v.tensor_scalar(pt_f32[:], iota_s[:], qcol_t[:], None, op0=ALU.is_equal)
                    v.bar()
                    v.tensor_copy(PT_sb[:, i * CAP:(i + 1) * CAP], pt_f32[:])
                    v.tensor_scalar(pw_f32[:], pt_f32[:], rwcol_t[:], None, op0=ALU.mult)
                    v.bar()
                    v.tensor_copy(PwT_sb[:], pw_f32[:]).then_inc(vS.sem, 1)
                    vS.n += 1
                    mark(f"pw_{i}", vS)

                @block.tensor
                def _(t, i=i):
                    t = TEng(t)
                    waitm(t, f"pw_{i}")
                    guard_wait(t, "psr5")
                    for jj in range(NJT):
                        t.transpose(ps_r5[:, jj * 128: jj * 128 + 128],
                                    PwT_sb[:, jj * 128: jj * 128 + 128], ident_s[:])
                    t.fin(tS.sem, 1)
                    tS.n += 1
                    mark(f"pwt_{i}", tS)
                    guard_set("PwT", tS)

                @block.vector
                def _(v, i=i):
                    v = Eng(v, vSelf)
                    waitm(v, f"pwt_{i}")
                    for jj in range(NJT):
                        v.tensor_copy(Pw_sb[:, (jj * 16 + i) * 128:(jj * 16 + i) * 128 + 128],
                                      ps_r5[:, jj * 128: jj * 128 + 128]).then_inc(vS.sem, 1)
                        vS.n += 1
                    mark(f"pwev_{i}", vS)
                    guard_set("psr5", vS)

            if debug:
                @block.gpsimd
                def _(gp):
                    if debug == 2 and "dgR" in mil:
                        waitm(gp, "dgR")
                        gp.dma_start(dbg["dbg_pT"], dgG_sb[:]).then_inc(dX.sem, 16)
                        dX.n += 16
                        dbg_written.update(["dbg_pT"])
                    waitm(gp, "pwev_15")
                    gp.dma_start(dbg["dbg_Q"], q16_sb[:]).then_inc(dX.sem, 16)
                    gp.dma_start(dbg["dbg_rw"], rw16_sb[:]).then_inc(dX.sem, 16)
                    gp.dma_start(dbg["dbg_lh"], lh_sb[:]).then_inc(dX.sem, 16)
                    gp.dma_start(dbg["dbg_cnt"], crow_sb[:]).then_inc(dX.sem, 16)
                    dX.n += 64
                    dbg_written.update(["dbg_Q", "dbg_rw", "dbg_lh", "dbg_cnt"])
            mark("p3_done", vS)

        if phases < 4:
            @block.vector
            def _(v):
                v = Eng(v, vSelf)
                waitm(v, "pwev_15")
            finish_early(dbg_written)
            return nc, dbg

        with ExitStack() as c4:
            E4 = c4.enter_context
            fxT_sb = E4(nc.sbuf_tensor("fxT_sb", [128, 16 * CAP], BF16))
            hT_sb = E4(nc.sbuf_tensor("hT_sb", [128, 64 * CAP], BF16))
            w1_sb = [E4(nc.sbuf_tensor(f"w1_sb{i}", [128, 16 * 512], BF16)) for i in range(2)]
            ol_sb = E4(nc.sbuf_tensor("ol_sb", [128, NJT * D], BF16))

            with ExitStack() as c4a:
                E4a = c4a.enter_context
                fx_sb = E4a(nc.sbuf_tensor("fx_sb", [128, NJT * D], BF16))
                x1g_f = [E4a(nc.sbuf_tensor(f"x1g_f{i}", [128, 1024], F32)) for i in range(2)]
                x1g_b = [E4a(nc.sbuf_tensor(f"x1g_b{i}", [128, 1024], BF16)) for i in range(2)]
                ps_g = [E4a(nc.psum_tensor(f"ps_g{j}", [128, 1024], F32)) for j in range(NJT)]
                ps_t4 = E4a(nc.psum_tensor("ps_t4", [128, 512], BF16))

                for half in range(2):
                    for i in range(16):
                        li = half * 16 + i

                        @block.sync
                        def _(sync, half=half, i=i, li=li):
                            if li == 0:
                                waitm(sync, "pwev_15")
                            waitm(sync, f"x1spill_{i}")
                            guard_wait(sync, f"x1g{li % 2}")
                            dma_seq(sync, dX, x1g_f[li % 2][:],
                                    x1_hbm.ap()[i * 128:(i + 1) * 128,
                                                half * 1024:(half + 1) * 1024])
                        mark(f"x1gl_{li}", dX)

                        @block.vector
                        def _(v, li=li):
                            v = Eng(v, vSelf)
                            waitm(v, f"x1gl_{li}")
                            guard_wait(v, f"x1gb{li % 2}")
                            v.tensor_copy(x1g_b[li % 2][:], x1g_f[li % 2][:]).then_inc(vS.sem, 1)
                            vS.n += 1
                            mark(f"x1gc_{li}", vS)
                            guard_set(f"x1g{li % 2}", vS)

                        @block.tensor
                        def _(t, half=half, i=i, li=li):
                            t = TEng(t)
                            waitm(t, f"x1gc_{li}")
                            waitm(t, "pwev_15")
                            if i == 0:
                                guard_wait(t, "ps_g")
                            for jj in range(NJT):
                                for dc in range(2):
                                    t.matmul(ps_g[jj][:, dc * 512: dc * 512 + 512],
                                             PT_sb[:, i * CAP + jj * 128: i * CAP + jj * 128 + 128],
                                             x1g_b[li % 2][:, dc * 512: dc * 512 + 512],
                                             start=(i == 0), stop=(i == 15))
                            t.fin(tS.sem, 1)
                            tS.n += 1
                            mark(f"fxmm_{half}_{i}", tS)
                            guard_set(f"x1gb{li % 2}", tS)

                    @block.vector
                    def _(v, half=half):
                        v = Eng(v, vSelf)
                        waitm(v, f"fxmm_{half}_15")
                        for jj in range(NJT):
                            v.tensor_copy(fx_sb[:, jj * D + half * 1024: jj * D + half * 1024 + 1024],
                                          ps_g[jj][:]).then_inc(vS.sem, 1)
                            vS.n += 1
                        mark(f"fxev_{half}", vS)
                        guard_set("ps_g", vS)

                for i in range(16):
                    @block.tensor
                    def _(t, i=i):
                        t = TEng(t)
                        waitm(t, "fxev_0")
                        waitm(t, "fxev_1")
                        guard_wait(t, "ps_t4")
                        for jj in range(NJT):
                            t.transpose(ps_t4[:, jj * 128: jj * 128 + 128],
                                        fx_sb[:, jj * D + i * 128: jj * D + i * 128 + 128], ident_s[:])
                        t.fin(tS.sem, 1)
                        tS.n += 1
                        mark(f"fxt_{i}", tS)

                    @block.vector
                    def _(v, i=i):
                        v = Eng(v, vSelf)
                        waitm(v, f"fxt_{i}")
                        for jj in range(NJT):
                            v.tensor_copy(fxT_sb[:, i * CAP + jj * 128: i * CAP + jj * 128 + 128],
                                          ps_t4[:, jj * 128: jj * 128 + 128]).then_inc(vS.sem, 1)
                            vS.n += 1
                        mark(f"fxtev_{i}", vS)
                        guard_set("ps_t4", vS)

            with ExitStack() as c4b:
                E4b = c4b.enter_context
                ps_h = [E4b(nc.psum_tensor(f"ps_h{j}", [128, CAP], F32)) for j in range(2)]

                for ch in range(16):
                    @block.gpsimd
                    def _(gp, ch=ch):
                        if ch < 2:
                            waitm(gp, "pwev_15")
                        guard_wait(gp, f"w1s{ch % 2}")
                        dma_seq(gp, dW, w1_sb[ch % 2][:].rearrange("p (a m) -> p a m", a=16),
                                w1_d[:, ch * 512:(ch + 1) * 512]
                                .rearrange("(a p) m -> p a m", p=128))
                    mark(f"w1l_{ch}", dW)

                    for fl in range(4):
                        ft = ch * 4 + fl

                        @block.tensor
                        def _(t, ch=ch, fl=fl, ft=ft):
                            t = TEng(t)
                            waitm(t, f"w1l_{ch}")
                            waitm(t, "fxtev_15")
                            guard_wait(t, f"ps_h{ft % 2}")
                            for i in range(16):
                                t.matmul(ps_h[ft % 2][:, 0:CAP],
                                         w1_sb[ch % 2][:, i * 512 + fl * 128: i * 512 + fl * 128 + 128],
                                         fxT_sb[:, i * CAP:(i + 1) * CAP],
                                         start=(i == 0), stop=(i == 15))
                            t.fin(tS.sem, 1)
                            tS.n += 1
                            mark(f"hmm_{ft}", tS)
                            if fl == 3:
                                guard_set(f"w1s{ch % 2}", tS)

                        @block.scalar
                        def _(s, ft=ft):
                            s = Eng(s, sSelf)
                            waitm(s, f"hmm_{ft}")
                            s.activation(hT_sb[:, ft * CAP:(ft + 1) * CAP], ps_h[ft % 2][:, 0:CAP],
                                         AF.Gelu, bias=b1_s[ft][:],
                                         scale=one_c[:]).then_inc(sS.sem, 1)
                            sS.n += 1
                            mark(f"gelu_{ft}", sS)
                            guard_set(f"ps_h{ft % 2}", sS)

                w2_sb = [E4b(nc.sbuf_tensor(f"w2_sb{i}", [128, 64 * 128], BF16)) for i in range(2)]
                olT_sb = E4b(nc.sbuf_tensor("olT_sb", [128, 16 * CAP], BF16))
                ps_t5 = E4b(nc.psum_tensor("ps_t5", [128, 512], BF16))

                for m in range(16):
                    @block.gpsimd
                    def _(gp, m=m):
                        if m < 2:
                            waitm(gp, "fxtev_15")
                        guard_wait(gp, f"w2s{m % 2}")
                        dma_seq(gp, dW, w2_sb[m % 2][:].rearrange("p (a m2) -> p a m2", a=64),
                                w2_d[:, m * 128:(m + 1) * 128]
                                .rearrange("(a p) m2 -> p a m2", p=128))
                    mark(f"w2l_{m}", dW)

                    @block.tensor
                    def _(t, m=m):
                        t = TEng(t)
                        waitm(t, f"w2l_{m}")
                        waitm(t, "gelu_63")
                        guard_wait(t, f"ps_h{m % 2}")
                        for f in range(64):
                            t.matmul(ps_h[m % 2][:, 0:CAP],
                                     w2_sb[m % 2][:, f * 128:(f + 1) * 128],
                                     hT_sb[:, f * CAP:(f + 1) * CAP],
                                     start=(f == 0), stop=(f == 63))
                        t.fin(tS.sem, 1)
                        tS.n += 1
                        mark(f"olmm_{m}", tS)
                        guard_set(f"w2s{m % 2}", tS)

                    @block.scalar
                    def _(s, m=m):
                        s = Eng(s, sSelf)
                        waitm(s, f"olmm_{m}")
                        s.activation(olT_sb[:, m * CAP:(m + 1) * CAP], ps_h[m % 2][:, 0:CAP],
                                     AF.Identity, bias=b2_s[m][:],
                                     scale=one_c[:]).then_inc(sS.sem, 1)
                        sS.n += 1
                        mark(f"olev_{m}", sS)
                        guard_set(f"ps_h{m % 2}", sS)

                for m in range(16):
                    @block.tensor
                    def _(t, m=m):
                        t = TEng(t)
                        waitm(t, f"olev_{m}")
                        guard_wait(t, "ps_t5")
                        for jj in range(NJT):
                            t.transpose(ps_t5[:, jj * 128: jj * 128 + 128],
                                        olT_sb[:, m * CAP + jj * 128: m * CAP + jj * 128 + 128],
                                        ident_s[:])
                        t.fin(tS.sem, 1)
                        tS.n += 1
                        mark(f"olt_{m}", tS)

                    @block.vector
                    def _(v, m=m):
                        v = Eng(v, vSelf)
                        waitm(v, f"olt_{m}")
                        for jj in range(NJT):
                            v.tensor_copy(ol_sb[:, jj * D + m * 128: jj * D + m * 128 + 128],
                                          ps_t5[:, jj * 128: jj * 128 + 128]).then_inc(vS.sem, 1)
                            vS.n += 1
                        mark(f"oltev_{m}", vS)
                        guard_set("ps_t5", vS)

                if debug:
                    @block.gpsimd
                    def _(gp):
                        waitm(gp, "fxev_1")
                        gp.dma_start(dbg["dbg_fx"], fx_sb[:, 0:D]).then_inc(dX.sem, 16)
                        waitm(gp, "gelu_0")
                        gp.dma_start(dbg["dbg_h0"], hT_sb[:, 0:CAP]).then_inc(dX.sem, 16)
                        waitm(gp, "oltev_15")
                        gp.dma_start(dbg["dbg_ol"], ol_sb[:, 0:D]).then_inc(dX.sem, 16)
                        dX.n += 48
                        dbg_written.update(["dbg_fx", "dbg_h0", "dbg_ol"])

            x1r_sb = [E4(nc.sbuf_tensor(f"x1r_sb{i}", [128, D], F32)) for i in range(2)]
            o_sb = [E4(nc.sbuf_tensor(f"o_sb{i}", [128, D], F32)) for i in range(2)]
            sq4_sb = E4(nc.sbuf_tensor("sq4_sb", [128, D], F32))
            st4_sb = E4(nc.sbuf_tensor("st4_sb", [128, 12], F32))
            sumsq4_t = E4(nc.sbuf_tensor("sumsq4_t", [128, 1], F32))
            rstd4_t = E4(nc.sbuf_tensor("rstd4_t", [128, 1], F32))
            m4_t = E4(nc.sbuf_tensor("m4_t", [128, 1], F32))
            ps_d4 = [E4(nc.psum_tensor(f"ps_d4{i}", [128, 2048], F32)) for i in range(2)]

            for gt in range(16):
                @block.sync
                def _(sync, gt=gt):
                    if gt < 2:
                        waitm(sync, "oltev_15")
                    guard_wait(sync, f"x1r{gt % 2}")
                    dma_seq(sync, dX, x1r_sb[gt % 2][:],
                            x1_hbm.ap()[gt * 128:(gt + 1) * 128, :])
                mark(f"x1rl_{gt}", dX)

                @block.tensor
                def _(t, gt=gt):
                    t = TEng(t)
                    waitm(t, "oltev_15")
                    guard_wait(t, f"ps_d4{gt % 2}")
                    for dc in range(4):
                        for jj in range(NJT):
                            t.matmul(ps_d4[gt % 2][:, dc * 512: dc * 512 + 512],
                                     Pw_sb[:, (jj * 16 + gt) * 128:(jj * 16 + gt) * 128 + 128],
                                     ol_sb[:, jj * D + dc * 512: jj * D + dc * 512 + 512],
                                     start=(jj == 0), stop=(jj == NJT - 1))
                    t.fin(tS.sem, 1)
                    tS.n += 1
                    mark(f"dmm_{gt}", tS)

                @block.vector
                def _(v, gt=gt):
                    v = Eng(v, vSelf)
                    waitm(v, f"dmm_{gt}")
                    waitm(v, f"x1rl_{gt}")
                    guard_wait(v, f"o{gt % 2}")
                    for dc in range(4):
                        v.tensor_add(o_sb[gt % 2][:, dc * 512: dc * 512 + 512],
                                     ps_d4[gt % 2][:, dc * 512: dc * 512 + 512],
                                     x1r_sb[gt % 2][:, dc * 512: dc * 512 + 512])
                    v.bar()
                    v.reduce_sum(st4_sb[:, 0:1], o_sb[gt % 2][:], axis=AX.X).then_inc(vS.sem, 1)
                    vS.n += 1
                    mark(f"x2_{gt}", vS)
                    guard_set(f"ps_d4{gt % 2}", vS)
                    guard_set(f"x1r{gt % 2}", vS)

                @block.scalar
                def _(s, gt=gt):
                    s = Eng(s, sSelf)
                    waitm(s, f"x2_{gt}")
                    guard_wait(s, "sq4")
                    s.activation(sq4_sb[:], o_sb[gt % 2][:], AF.Square,
                                 accum_out=sumsq4_t[:]).then_inc(sS.sem, 1)
                    sS.n += 1
                    mark(f"sumsq4_{gt}", sS)

                @block.vector
                def _(v, gt=gt):
                    v = Eng(v, vSelf)
                    waitm(v, f"sumsq4_{gt}")
                    v.tensor_scalar(m4_t[:], st4_sb[:, 0:1], 1.0 / D, None, op0=ALU.mult)
                    v.tensor_scalar(st4_sb[:, 3:4], sumsq4_t[:], 1.0 / D, None, op0=ALU.mult)
                    v.bar()
                    v.tensor_mul(st4_sb[:, 4:5], m4_t[:], m4_t[:])
                    v.bar()
                    v.tensor_sub(st4_sb[:, 5:6], st4_sb[:, 3:4], st4_sb[:, 4:5])
                    v.bar()
                    v.tensor_scalar(st4_sb[:, 5:6], st4_sb[:, 5:6], 0.0, None, op0=ALU.max).then_inc(vS.sem, 1)
                    vS.n += 1
                    mark(f"var4_{gt}", vS)
                    guard_set("sq4", vS)

                @block.scalar
                def _(s, gt=gt):
                    s = Eng(s, sSelf)
                    waitm(s, f"var4_{gt}")
                    s.activation(st4_sb[:, 6:7], st4_sb[:, 5:6], AF.Ln, bias=eps_c[:], scale=one_c[:])
                    s.bar()
                    s.activation(rstd4_t[:], st4_sb[:, 6:7], AF.Exp, bias=zero_c[:],
                                 scale=nhalf_c[:]).then_inc(sS.sem, 1)
                    sS.n += 1
                    mark(f"rstd4_{gt}", sS)

                @block.vector
                def _(v, gt=gt):
                    v = Eng(v, vSelf)
                    waitm(v, f"rstd4_{gt}")
                    v.tensor_scalar(o_sb[gt % 2][:], o_sb[gt % 2][:], m4_t[:], rstd4_t[:],
                                    op0=ALU.subtract, op1=ALU.mult).then_inc(vS.sem, 1)
                    vS.n += 1
                    mark(f"o_{gt}", vS)

                @block.sync
                def _(sync, gt=gt):
                    waitm(sync, f"o_{gt}")
                    sync.dma_start(out_d[gt * 128:(gt + 1) * 128, :],
                                   o_sb[gt % 2][:]).then_inc(dO.sem, 16)
                    dO.n += 16
                mark(f"ostore_{gt}", dO)
                guard_set(f"o{gt % 2}", dO)

            @block.sync
            def _(sync):
                sync.wait_ge(dO.sem, dO.n)

    return nc, dbg


def _stage_inputs(inputs):
    x = _f32(inputs["L0_attn_x"])
    Wq = _f32(inputs["L0_attn_Wq"])
    Wk = _f32(inputs["L0_attn_Wk"])
    Wv = _f32(inputs["L0_attn_Wv"])
    Wo = _f32(inputs["L0_attn_Wo"])
    Wr = _f32(inputs["L0_ffn_Wr"])
    br = _f32(inputs["L0_ffn_br"])
    W1 = _f32(inputs["L0_ffn_W1"])
    b1 = _f32(inputs["L0_ffn_b1"])
    W2 = _f32(inputs["L0_ffn_W2"])
    b2 = _f32(inputs["L0_ffn_b2"])

    perm = np.concatenate([hh * DH + np.concatenate([np.arange(0, DH, 2), np.arange(1, DH, 2)])
                           for hh in range(H)])
    wqT = _f32(Wq[perm, :].T)
    wkT = _f32(Wk[perm, :].T)
    wvT = _f32(Wv.T)
    woT = _f32(Wo.T)

    wr_rep = _f32(np.broadcast_to(Wr[:, 0][None, :], (128, D)))
    b1c = _f32(b1.reshape(DFF // 128, 128))
    b2c = _f32(b2.reshape(D // 128, 128))
    br_rep = _f32(np.broadcast_to(br.reshape(1, 1), (128, 1)))

    ident_b = _bf(np.eye(128))
    psw = np.zeros((128, 128), np.float32)
    for p in range(128):
        psw[(p + 64) % 128, p] = 1.0
    psw_b = _bf(psw)
    rr = np.arange(128)
    tri = np.where(rr[None, :] <= rr[:, None], 0.0, -1e30).astype(np.float32)
    trimask_b = _bf(tri)
    triu_b = _bf(np.triu(np.ones((128, 128), np.float32)))
    iota384 = _f32(np.broadcast_to(np.arange(1, CAP + 1, dtype=np.float32)[None, :], (128, CAP)))
    bm16 = _f32((rr[:, None] // 8 == np.arange(NPROBE)[None, :]).astype(np.float32))
    step_col = _f32(((rr // 8 + 1) / (NPROBE + 1.0))[:, None])
    step_row = _f32(((np.arange(NPROBE) + 1) / (NPROBE + 1.0))[None, :])
    repl8 = _f32((np.arange(8)[:, None] == (rr[None, :] % 8)).astype(np.float32))
    ones8 = _f32(np.ones((8, 1)))
    ones128 = _f32(np.ones((128, 1)))
    onesr = _f32(np.ones((1, 128)))

    inv = 1.0 / (10000.0 ** (np.arange(0, DH, 2, dtype=np.float32) / DH))
    s128 = 1.0 / np.sqrt(DH)

    per_core = []
    for c in range(NCORE):
        b = c // 4
        qd = c % 4
        t0 = qd * T_CORE
        xc = _f32(x[b, t0:t0 + T_CORE, :])
        xTfull = np.ascontiguousarray(x[b].T)
        if qd == 0:
            halo = np.zeros((D, HALO), np.float32)
        else:
            halo = xTfull[:, t0 - HALO:t0]
        xTc = _f32(np.concatenate([halo, xTfull[:, t0:t0 + T_CORE]], axis=1))

        tq = t0 + np.arange(T_CORE, dtype=np.float64)
        tk = np.maximum(t0 - HALO + np.arange(T_K, dtype=np.float64), 0.0)
        angq = np.outer(inv.astype(np.float64), tq)
        angk = np.outer(inv.astype(np.float64), tk)
        c2q = _bf(np.concatenate([np.cos(angq), np.cos(angq)], axis=0) * s128)
        s2q = _bf(np.concatenate([-np.sin(angq), np.sin(angq)], axis=0) * s128)
        c2k = _bf(np.concatenate([np.cos(angk), np.cos(angk)], axis=0))
        s2k = _bf(np.concatenate([-np.sin(angk), np.sin(angk)], axis=0))

        w0prev = (np.zeros((128, 256), np.float32) if qd != 0
                  else np.full((128, 256), -1e30, np.float32))

        per_core.append({
            "x": xc, "xT": xTc, "wqT": wqT, "wkT": wkT, "wvT": wvT, "woT": woT,
            "w1": W1, "w2": W2, "wr_rep": wr_rep, "b1c": b1c, "b2c": b2c, "br_rep": br_rep,
            "ident_b": ident_b, "psw_b": psw_b, "trimask_b": trimask_b,
            "w0prev_b": _bf(w0prev), "triu_b": triu_b, "iota384": iota384, "bm16": bm16,
            "step_col": step_col, "step_row": step_row, "repl8": repl8, "ones8": ones8,
            "ones128": ones128, "onesr_f32": onesr,
            "c2q": c2q, "s2q": s2q, "c2k": c2k, "s2k": s2k,
        })
    return per_core


def kernel(**inputs):
    per_core = _stage_inputs(inputs)
    nc, _ = build(debug=False)
    res = run_bass_kernel_spmd(nc, per_core, core_ids=list(range(NCORE)))
    full = np.empty((2, 8192, D), np.float32)
    for c in range(NCORE):
        full[c // 4, (c % 4) * T_CORE:(c % 4 + 1) * T_CORE, :] = res.results[c]["out"]
    return full


# version 09
# speedup vs baseline: 1.0137x; 1.0137x over previous
import numpy as np
import ml_dtypes
from contextlib import ExitStack

import concourse.bass as bass
import concourse.mybir as mybir
from concourse.bass_utils import run_bass_kernel_spmd

F32 = mybir.dt.float32
BF16 = mybir.dt.bfloat16
AF = mybir.ActivationFunctionType
ALU = mybir.AluOpType
AX = mybir.AxisListType

NCORE = 8
T_CORE = 2048
HALO = 256
T_K = T_CORE + HALO
D = 2048
H = 16
DH = 128
NW = 8
DFF = 8192
CAP = 384
NJT = CAP // 128
KTOP = 1024
SHIFT = 20.0
RSHIFT = 16.0
NROUND = 10
NPROBE = 16

BF = ml_dtypes.bfloat16


def _f32(a):
    return np.ascontiguousarray(np.asarray(a), dtype=np.float32)


def _bf(a):
    return np.ascontiguousarray(np.asarray(a, dtype=np.float32).astype(BF))



class TEng:
    def __init__(self, t):
        self.t = t
        self.last = None

    def matmul(self, *a, **k):
        self.last = self.t.matmul(*a, **k)
        return self.last

    def transpose(self, *a, **k):
        self.last = self.t.transpose(*a, **k)
        return self.last

    def wait_ge(self, *a, **k):
        return self.t.wait_ge(*a, **k)

    def fin(self, sem, n=1):
        self.last.then_inc(sem, n)



class Eng:
    def __init__(self, e, ctr):
        self.e = e
        self.ctr = ctr
        self.last = None

    def __getattr__(self, name):
        f = getattr(self.e, name)
        if callable(f) and name not in ("wait_ge", "engine_nop"):
            def g(*a, **k):
                r = f(*a, **k)
                if hasattr(r, "then_inc"):
                    self.last = r
                return r
            return g
        return f

    def bar(self):
        self.last.then_inc(self.ctr.sem, 1)
        self.ctr.n += 1
        self.e.wait_ge(self.ctr.sem, self.ctr.n)


class Ctr:
    __slots__ = ("sem", "n")

    def __init__(self, sem):
        self.sem = sem
        self.n = 0


def build(debug=False, phases=4, ngroups=8, p1_upto=99):
    nc = bass.Bass("TRN2", target_bir_lowering=False)

    def din(name, shape, dt=F32):
        return nc.dram_tensor(name, shape, dt, kind="ExternalInput").ap()

    x_d = din("x", [T_CORE, D])
    xT_d = din("xT", [D, T_K])
    wq_d = din("wqT", [D, D])
    wk_d = din("wkT", [D, D])
    wv_d = din("wvT", [D, D])
    wo_d = din("woT", [D, D])
    w1_d = din("w1", [D, DFF])
    w2_d = din("w2", [DFF, D])
    wr_d = din("wr_rep", [128, D])
    b1_d = din("b1c", [DFF // 128, 128])
    b2_d = din("b2c", [D // 128, 128])
    br_d = din("br_rep", [128, 1])
    ident_d = din("ident_b", [128, 128], BF16)
    psw_d = din("psw_b", [128, 128], BF16)
    tri_d = din("trimask_b", [128, 128], BF16)
    w0p_d = din("w0prev_b", [128, 256], BF16)
    triu_d = din("triu_b", [128, 128], BF16)
    iota_d = din("iota384", [128, CAP])
    bm_d = din("bm16", [128, NPROBE])
    stepc_d = din("step_col", [128, 1])
    stepr_d = din("step_row", [1, NPROBE])
    repl_d = din("repl8", [8, 128])
    ones8_d = din("ones8", [8, 1])
    ones128_d = din("ones128", [128, 1])
    onesr_d = din("onesr_f32", [1, 128])
    c2q_d = din("c2q", [128, T_CORE], BF16)
    s2q_d = din("s2q", [128, T_CORE], BF16)
    c2k_d = din("c2k", [128, T_K], BF16)
    s2k_d = din("s2k", [128, T_K], BF16)

    out_d = nc.dram_tensor("out", [T_CORE, D], F32, kind="ExternalOutput").ap()

    dbg = {}
    if debug:
        for name, shape in [
            ("dbg_q0", [128, T_CORE]), ("dbg_k0", [128, T_K]),
            ("dbg_v0", [128, 256]), ("dbg_ao0", [128, T_CORE]),
            ("dbg_x1a", [128, D]), ("dbg_logit", [128, 16]),
            ("dbg_lh", [1, 8]), ("dbg_Q", [128, 16]),
            ("dbg_rw", [128, 16]), ("dbg_fx", [128, D]),
            ("dbg_h0", [128, CAP]), ("dbg_ol", [128, D]),
            ("dbg_cnt", [1, NPROBE]),
            ("dbg_psA", [128, 512]), ("dbg_psB", [128, 512]),
            ("dbg_probs", [128, 1024]), ("dbg_pT", [128, 1024]),
        ]:
            dbg[name] = nc.dram_tensor(name, shape, F32, kind="ExternalOutput").ap()

    ao_hbm = nc.dram_tensor("ao_hbm", [D, T_CORE], BF16)
    x1_hbm = nc.dram_tensor("x1_hbm", [T_CORE, D], F32)
    agin = nc.dram_tensor("agin", [1, T_CORE], F32)
    agout = nc.dram_tensor("agout", [4, T_CORE], F32)

    mil = {}
    dbg_written = set()
    guards = {}

    with ExitStack() as ctx:
        E = ctx.enter_context

        ident_s = E(nc.sbuf_tensor("ident_s", [128, 128], BF16))
        psw_s = E(nc.sbuf_tensor("psw_s", [128, 128], BF16))
        tri_s = E(nc.sbuf_tensor("tri_s", [128, 128], BF16))
        w0p_s = E(nc.sbuf_tensor("w0p_s", [128, 256], BF16))
        triu_s = E(nc.sbuf_tensor("triu_s", [128, 128], BF16))
        iota_s = E(nc.sbuf_tensor("iota_s", [128, CAP], F32))
        bm_s = E(nc.sbuf_tensor("bm_s", [128, NPROBE], F32))
        stepc_s = E(nc.sbuf_tensor("stepc_s", [128, 1], F32))
        stepr_s = E(nc.sbuf_tensor("stepr_s", [1, NPROBE], F32))
        repl_s = E(nc.sbuf_tensor("repl_s", [8, 128], F32))
        ones8_s = E(nc.sbuf_tensor("ones8_s", [8, 1], F32))
        ones128_s = E(nc.sbuf_tensor("ones128_s", [128, 1], F32))
        onesr_s = E(nc.sbuf_tensor("onesr_s", [1, 128], F32))
        c2q_s = E(nc.sbuf_tensor("c2q_s", [128, T_CORE], BF16))
        s2q_s = E(nc.sbuf_tensor("s2q_s", [128, T_CORE], BF16))
        c2k_s = E(nc.sbuf_tensor("c2k_s", [128, T_K], BF16))
        s2k_s = E(nc.sbuf_tensor("s2k_s", [128, T_K], BF16))
        b1_s = [E(nc.sbuf_tensor(f"b1_s{i}", [128, 1], F32)) for i in range(DFF // 128)]
        b2_s = [E(nc.sbuf_tensor(f"b2_s{i}", [128, 1], F32)) for i in range(D // 128)]
        br_s = E(nc.sbuf_tensor("br_s", [128, 1], F32))
        wr_f = E(nc.sbuf_tensor("wr_f", [128, D], F32))
        zero_c = E(nc.sbuf_tensor("zero_c", [128, 1], F32))
        one_c = E(nc.sbuf_tensor("one_c", [128, 1], F32))
        mshift_c = E(nc.sbuf_tensor("mshift_c", [128, 1], F32))
        mrs_c = E(nc.sbuf_tensor("mrs_c", [128, 1], F32))
        eps_c = E(nc.sbuf_tensor("eps_c", [128, 1], F32))
        nhalf_c = E(nc.sbuf_tensor("nhalf_c", [128, 1], F32))
        logit_sb = E(nc.sbuf_tensor("logit_sb", [128, 16], F32))

        sems = {}
        for name in ["dW", "dX", "dW2", "dX2", "dSp", "dO", "tS", "vS", "sS", "gS", "cc", "vSelf", "sSelf"]:
            sems[name] = Ctr(E(nc.semaphore(name)))
        dW, dX, dSp, dO = sems["dW"], sems["dX"], sems["dSp"], sems["dO"]
        dW2, dX2 = sems["dW2"], sems["dX2"]
        tS, vS, sS, gS, cc = sems["tS"], sems["vS"], sems["sS"], sems["gS"], sems["cc"]
        vSelf, sSelf = sems["vSelf"], sems["sSelf"]

        block = E(nc.Block())


        def finish_early(written):
            zb = ctx.enter_context(nc.sbuf_tensor("zb_fill", [128, T_K], F32))
            @block.gpsimd
            def _(gp):
                gp.memset(zb[:], 0.0).then_inc(gS.sem, 1)
                gS.n += 1
            zf = gS.n

            @block.sync
            def _(sync):
                sync.wait_ge(gS.sem, zf)
                for gt in range(16):
                    sync.dma_start(out_d[gt * 128:(gt + 1) * 128, :], zb[:, 0:D]).then_inc(dO.sem, 16)
                    dO.n += 16
                for name, ap in dbg.items():
                    if name in written:
                        continue
                    pp, ff = ap.tensor.shape if hasattr(ap, "tensor") else (128, D)
                    sync.dma_start(ap, zb[0:pp, 0:ff]).then_inc(dO.sem, 16)
                    dO.n += 16
                sync.wait_ge(dO.sem, dO.n)


        def dma_seq(eng, ctr, out, in_):
            if ctr.n > 0:
                eng.wait_ge(ctr.sem, ctr.n)
            eng.dma_start(out, in_).then_inc(ctr.sem, 16)
            ctr.n += 16

        def mark(name, ctr):
            mil[name] = (ctr, ctr.n)

        waited = {}

        def eng_key(eng):
            if isinstance(eng, TEng):
                return "PE"
            if isinstance(eng, Eng):
                return "V" if eng.ctr.sem is sems["vSelf"].sem else "S"
            en = getattr(eng, "engine", None)
            return str(en)

        def _wait(eng, sem_obj, val):
            k = (eng_key(eng), id(sem_obj))
            if waited.get(k, -1) >= val:
                return
            waited[k] = val
            eng.wait_ge(sem_obj, val)

        def waitm(eng, name):
            ctr, val = mil[name]
            if val > 0:
                _wait(eng, ctr.sem, val)

        def guard_set(res, ctr):
            guards[res] = (ctr, ctr.n)

        def guard_wait(eng, res):
            if res in guards:
                ctr, val = guards[res]
                if val > 0:
                    _wait(eng, ctr.sem, val)

        @block.sync
        def _(sync):
            for dst, src in [
                (ident_s, ident_d), (psw_s, psw_d), (tri_s, tri_d), (w0p_s, w0p_d),
                (triu_s, triu_d), (iota_s, iota_d), (bm_s, bm_d), (stepc_s, stepc_d),
                (stepr_s, stepr_d), (repl_s, repl_d), (ones8_s, ones8_d),
                (ones128_s, ones128_d), (onesr_s, onesr_d), (c2q_s, c2q_d),
                (s2q_s, s2q_d), (c2k_s, c2k_d), (s2k_s, s2k_d),
                (br_s, br_d), (wr_f, wr_d),
            ]:
                dma_seq(sync, dX, dst[:], src)
            b1v = b1_d.rearrange("a (p c) -> a p c", c=1)
            b2v = b2_d.rearrange("a (p c) -> a p c", c=1)
            for i in range(DFF // 128):
                dma_seq(sync, dX, b1_s[i][:], b1v[i])
            for i in range(D // 128):
                dma_seq(sync, dX, b2_s[i][:], b2v[i])
        mark("consts", dX)

        @block.gpsimd
        def _(gp):
            gp.memset(zero_c[:], 0.0)
            gp.memset(one_c[:], 1.0)
            gp.memset(mshift_c[:], -SHIFT)
            gp.memset(mrs_c[:], -RSHIFT)
            gp.memset(eps_c[:], 1e-5)
            gp.memset(nhalf_c[:], -0.5).then_inc(gS.sem, 1)
            gS.n += 1
        mark("memset", gS)

        with ExitStack() as c1:
            E1 = c1.enter_context
            xT_sb = E1(nc.sbuf_tensor("xT_sb", [128, 16 * T_K], BF16))
            wq_sb = [E1(nc.sbuf_tensor(f"wq_sb{i}", [128, 16 * 256], BF16)) for i in range(2)]
            wk_sb = [E1(nc.sbuf_tensor(f"wk_sb{i}", [128, 16 * 256], BF16)) for i in range(2)]
            wv_sb = [E1(nc.sbuf_tensor(f"wv_sb{i}", [128, 16 * 256], BF16)) for i in range(2)]
            qraw_sb = E1(nc.sbuf_tensor("qraw_sb", [128, T_K], BF16))
            qsw_sb = E1(nc.sbuf_tensor("qsw_sb", [128, T_K], BF16))
            qh_sb = [E1(nc.sbuf_tensor(f"qh_sb{i}", [128, T_CORE], BF16)) for i in range(2)]
            kh_sb = [E1(nc.sbuf_tensor(f"kh_sb{i}", [128, T_K], BF16)) for i in range(2)]
            v_sb = E1(nc.sbuf_tensor("v_sb", [128, 18 * 256], BF16))
            probs_sb = E1(nc.sbuf_tensor("probs_sb", [128, 1024], BF16))
            probsF_sb = E1(nc.sbuf_tensor("probsF_sb", [128, 1024], F32))
            probsN_sb = E1(nc.sbuf_tensor("probsN_sb", [128, 1024], F32))
            recipA_t = E1(nc.sbuf_tensor("recipA_t", [128, 1], F32))
            recipB_t = E1(nc.sbuf_tensor("recipB_t", [128, 1], F32))
            probsT_sb = E1(nc.sbuf_tensor("probsT_sb", [128, 1024], BF16))
            sums_sb = E1(nc.sbuf_tensor("sums_sb", [128, 2], F32))
            sumA_t = E1(nc.sbuf_tensor("sumA_t", [128, 1], F32))
            sumB_t = E1(nc.sbuf_tensor("sumB_t", [128, 1], F32))
            recip_sb = E1(nc.sbuf_tensor("recip_sb", [128, 2], F32))
            ao_sb = [E1(nc.sbuf_tensor(f"ao_sb{i}", [128, T_CORE], BF16)) for i in range(2)]

            ps_pj = [E1(nc.psum_tensor(f"ps_pj{i}", [128, 512], F32)) for i in range(2)]
            ps_sw = [E1(nc.psum_tensor(f"ps_sw{i}", [128, 512], F32)) for i in range(2)]
            ps_scA = E1(nc.psum_tensor("ps_scA", [128, 512], F32))
            ps_scB = E1(nc.psum_tensor("ps_scB", [128, 512], F32))
            ps_pt = E1(nc.psum_tensor("ps_pt", [128, 1024], BF16))
            ps_av = E1(nc.psum_tensor("ps_av", [128, 256], F32))

            @block.gpsimd
            def _(gp):
                dma_seq(gp, dW, xT_sb[:].rearrange("p (a m) -> p a m", a=16),
                        xT_d.rearrange("(a p) m -> p a m", p=128))
            mark("xT", dW)

            def emit_wload(g):
                @block.gpsimd
                def _(gp, g=g):
                    guard_wait(gp, f"wslot{g % 2}")
                    wctr = dW if g % 2 == 0 else dW2
                    for w_dd, w_ss in ((wq_d, wq_sb), (wk_d, wk_sb), (wv_d, wv_sb)):
                        dma_seq(gp, wctr,
                                w_ss[g % 2][:].rearrange("p (a m) -> p a m", a=16),
                                w_dd[:, g * 256:(g + 1) * 256].rearrange("(a p) m -> p a m", p=128))
                mark(f"wg_{g}", dW if g % 2 == 0 else dW2)

            def emit_proj_rope(g, hh, which):
                h = 2 * g + hh
                sl = g % 2
                if which == "q":
                    w_sb_, tab_c, tab_s, out_sb, out_res, ncols, col0 = \
                        wq_sb, c2q_s, s2q_s, qh_sb[hh], f"qh{hh}", T_CORE, 256
                else:
                    w_sb_, tab_c, tab_s, out_sb, out_res, ncols, col0 = \
                        wk_sb, c2k_s, s2k_s, kh_sb[hh], f"kh{hh}", T_K, 0
                nch = (ncols + 511) // 512

                def pj_stage(c):
                    n = min(512, ncols - c * 512)

                    @block.tensor
                    def _(t, c=c, n=n):
                        t = TEng(t)
                        if c == 0:
                            waitm(t, f"wg_{g}")
                            waitm(t, "xT")
                            guard_wait(t, "qraw")
                        guard_wait(t, f"pj{c % 2}")
                        for i in range(16):
                            t.matmul(ps_pj[c % 2][:, 0:n],
                                     w_sb_[sl][:, i * 256 + hh * 128: i * 256 + hh * 128 + 128],
                                     xT_sb[:, i * T_K + col0 + c * 512: i * T_K + col0 + c * 512 + n],
                                     start=(i == 0), stop=(i == 15))
                        t.fin(tS.sem, 1)
                        tS.n += 1
                        mark(f"pj_{which}_{h}_{c}", tS)

                    @block.vector
                    def _(v, c=c, n=n):
                        v = Eng(v, vSelf)
                        waitm(v, f"pj_{which}_{h}_{c}")
                        v.tensor_copy(qraw_sb[:, c * 512: c * 512 + n],
                                      ps_pj[c % 2][:, 0:n]).then_inc(vS.sem, 1)
                        vS.n += 1
                        mark(f"evraw_{which}_{h}_{c}", vS)
                        guard_set(f"pj{c % 2}", vS)

                def sw_stage(c):
                    n = min(512, ncols - c * 512)

                    @block.tensor
                    def _(t, c=c, n=n):
                        t = TEng(t)
                        if c == 0:
                            guard_wait(t, "qsw")
                        waitm(t, "consts")
                        waitm(t, f"evraw_{which}_{h}_{c}")
                        guard_wait(t, f"sw{c % 2}")
                        t.matmul(ps_sw[c % 2][:, 0:n], psw_s[:], qraw_sb[:, c * 512: c * 512 + n],
                                 start=True, stop=True)
                        t.fin(tS.sem, 1)
                        tS.n += 1
                        mark(f"sw_{which}_{h}_{c}", tS)

                    @block.vector
                    def _(v, c=c, n=n):
                        v = Eng(v, vSelf)
                        waitm(v, f"sw_{which}_{h}_{c}")
                        v.tensor_copy(qsw_sb[:, c * 512: c * 512 + n],
                                      ps_sw[c % 2][:, 0:n]).then_inc(vS.sem, 1)
                        vS.n += 1
                        mark(f"evsw_{which}_{h}_{c}", vS)
                        guard_set(f"sw{c % 2}", vS)

                pj_stage(0)
                for c in range(1, nch):
                    pj_stage(c)
                    sw_stage(c - 1)
                sw_stage(nch - 1)

                @block.vector
                def _(v):
                    v = Eng(v, vSelf)
                    waitm(v, f"evraw_{which}_{h}_{nch - 1}")
                    waitm(v, f"evsw_{which}_{h}_{nch - 1}")
                    guard_wait(v, out_res)
                    v.tensor_mul(qraw_sb[:, 0:ncols], qraw_sb[:, 0:ncols], tab_c[:, 0:ncols])
                    v.tensor_mul(qsw_sb[:, 0:ncols], qsw_sb[:, 0:ncols], tab_s[:, 0:ncols])
                    v.bar()
                    v.tensor_add(out_sb[:, 0:ncols], qraw_sb[:, 0:ncols], qsw_sb[:, 0:ncols]).then_inc(vS.sem, 1)
                    vS.n += 1
                    mark(f"rope_{which}_{h}", vS)
                    guard_set("qraw", vS)
                    guard_set("qsw", vS)

            def emit_vproj(g):
                sl = g % 2
                for jt in range(18):
                    @block.tensor
                    def _(t, jt=jt):
                        t = TEng(t)
                        if jt == 0:
                            waitm(t, f"wg_{g}")
                            waitm(t, "xT")
                        guard_wait(t, f"pj{jt % 2}")
                        for i in range(16):
                            t.matmul(ps_pj[jt % 2][:, 0:256],
                                     xT_sb[:, i * T_K + jt * 128: i * T_K + jt * 128 + 128],
                                     wv_sb[sl][:, i * 256: i * 256 + 256],
                                     start=(i == 0), stop=(i == 15))
                        t.fin(tS.sem, 1)
                        tS.n += 1
                        mark(f"vpj_{g}_{jt}", tS)

                    @block.vector
                    def _(v, jt=jt):
                        v = Eng(v, vSelf)
                        if jt == 0:
                            guard_wait(v, "v_sb")
                        waitm(v, f"vpj_{g}_{jt}")
                        v.tensor_copy(v_sb[:, jt * 256: jt * 256 + 256],
                                      ps_pj[jt % 2][:, 0:256]).then_inc(vS.sem, 1)
                        vS.n += 1
                        mark(f"evv_{g}_{jt}", vS)
                        guard_set(f"pj{jt % 2}", vS)

            def emit_attention(g, hh):
                h = 2 * g + hh
                qh, kh = qh_sb[hh], kh_sb[hh]

                def sc_stage(w):
                    @block.tensor
                    def _(t, w=w):
                        t = TEng(t)
                        if w == 0:
                            waitm(t, f"rope_q_{h}")
                            waitm(t, f"rope_k_{h}")
                            waitm(t, f"evv_{g}_17")
                            waitm(t, "consts")
                        guard_wait(t, "scA")
                        guard_wait(t, "scB")
                        t.matmul(ps_scA[:, 256:384], ident_s[:], tri_s[:], start=True, stop=True)
                        t.matmul(ps_scB[:, 384:512], ident_s[:], tri_s[:], start=True, stop=True)
                        if w == 0:
                            t.matmul(ps_scA[:, 0:256], ident_s[:], w0p_s[:], start=False, stop=True)
                            t.matmul(ps_scB[:, 0:256], ident_s[:], w0p_s[:], start=False, stop=True)
                        t.matmul(ps_scA[:, 0:384], qh[:, w * 256: w * 256 + 128],
                                 kh[:, w * 256: w * 256 + 384], start=False, stop=True)
                        t.matmul(ps_scB[:, 0:512], qh[:, w * 256 + 128: w * 256 + 256],
                                 kh[:, w * 256: w * 256 + 512], start=False, stop=True)
                        t.fin(tS.sem, 1)
                        tS.n += 1
                        mark(f"score_{h}_{w}", tS)

                    @block.scalar
                    def _(s, w=w):
                        s = Eng(s, sSelf)
                        waitm(s, f"score_{h}_{w}")
                        guard_wait(s, "probsF")
                        s.activation(probsF_sb[:, 0:384], ps_scA[:, 0:384], AF.Exp,
                                     bias=mshift_c[:], scale=one_c[:])
                        s.activation(probsF_sb[:, 512:1024], ps_scB[:, 0:512], AF.Exp,
                                     bias=mshift_c[:], scale=one_c[:]).then_inc(sS.sem, 1)
                        sS.n += 1
                        mark(f"exp_{h}_{w}", sS)
                        guard_set("scA", sS)
                        guard_set("scB", sS)

                    @block.vector
                    def _(v, w=w):
                        v = Eng(v, vSelf)
                        waitm(v, f"exp_{h}_{w}")
                        v.reduce_sum(sumA_t[:], probsF_sb[:, 0:384], axis=AX.X)
                        v.reduce_sum(sumB_t[:], probsF_sb[:, 512:1024], axis=AX.X)
                        v.bar()
                        v.tensor_scalar(sumA_t[:], sumA_t[:], 1e-30, None, op0=ALU.max)
                        v.tensor_scalar(sumB_t[:], sumB_t[:], 1e-30, None, op0=ALU.max)
                        v.bar()
                        v.reciprocal(recipA_t[:], sumA_t[:])
                        v.reciprocal(recipB_t[:], sumB_t[:])
                        v.bar()
                        guard_wait(v, "probs")
                        v.tensor_scalar(probsN_sb[:, 0:384], probsF_sb[:, 0:384],
                                        recipA_t[:], None, op0=ALU.mult)
                        v.tensor_scalar(probsN_sb[:, 512:1024], probsF_sb[:, 512:1024],
                                        recipB_t[:], None, op0=ALU.mult)
                        v.bar()
                        v.tensor_copy(probs_sb[:, 0:384], probsN_sb[:, 0:384])
                        v.tensor_copy(probs_sb[:, 512:1024], probsN_sb[:, 512:1024]).then_inc(vS.sem, 1)
                        vS.n += 1
                        mark(f"norm_{h}_{w}", vS)
                        guard_set("probsF", vS)

                def av_stage(w):
                    @block.tensor
                    def _(t, w=w):
                        t = TEng(t)
                        waitm(t, f"norm_{h}_{w}")
                        guard_wait(t, "ptv")
                        for j in range(3):
                            t.transpose(ps_pt[:, j * 256: j * 256 + 128],
                                        probs_sb[:, j * 128: j * 128 + 128], ident_s[:])
                        for j in range(4):
                            t.transpose(ps_pt[:, j * 256 + 128: j * 256 + 256],
                                        probs_sb[:, 512 + j * 128: 512 + j * 128 + 128], ident_s[:])
                        t.fin(tS.sem, 1)
                        tS.n += 1
                        mark(f"trp_{h}_{w}", tS)
                        guard_set("probs", tS)

                    @block.vector
                    def _(v, w=w):
                        v = Eng(v, vSelf)
                        waitm(v, f"trp_{h}_{w}")
                        guard_wait(v, "probsT")
                        for j in range(3):
                            v.tensor_copy(probsT_sb[:, j * 256: j * 256 + 128],
                                          ps_pt[:, j * 256: j * 256 + 128])
                            v.tensor_copy(probsT_sb[:, j * 256 + 128: j * 256 + 256],
                                          ps_pt[:, j * 256 + 128: j * 256 + 256])
                        v.tensor_copy(probsT_sb[:, 3 * 256 + 128: 4 * 256],
                                      ps_pt[:, 3 * 256 + 128: 4 * 256]).then_inc(vS.sem, 1)
                        vS.n += 1
                        mark(f"ptv_{h}_{w}", vS)

                    @block.tensor
                    def _(t, w=w):
                        t = TEng(t)
                        waitm(t, f"ptv_{h}_{w}")
                        guard_wait(t, "av")
                        for j in range(4):
                            kt = 2 * w + j
                            if j < 3:
                                t.matmul(ps_av[:, 0:256],
                                         v_sb[:, kt * 256 + hh * 128: kt * 256 + hh * 128 + 128],
                                         probsT_sb[:, j * 256: j * 256 + 256],
                                         start=(j == 0), stop=False)
                            else:
                                t.matmul(ps_av[:, 128:256],
                                         v_sb[:, kt * 256 + hh * 128: kt * 256 + hh * 128 + 128],
                                         probsT_sb[:, j * 256 + 128: j * 256 + 256],
                                         start=False, stop=True)
                        t.fin(tS.sem, 1)
                        tS.n += 1
                        mark(f"av_{h}_{w}", tS)
                        guard_set("probsT", tS)

                    @block.vector
                    def _(v, w=w):
                        v = Eng(v, vSelf)
                        waitm(v, f"av_{h}_{w}")
                        if w == 0:
                            guard_wait(v, f"ao{hh}")
                        v.tensor_copy(ao_sb[hh][:, w * 256: w * 256 + 256],
                                      ps_av[:, 0:256]).then_inc(vS.sem, 1)
                        vS.n += 1
                        mark(f"aoev_{h}_{w}", vS)
                        guard_set("av", vS)

                sc_stage(0)
                for w in range(1, NW):
                    sc_stage(w)
                    av_stage(w - 1)
                av_stage(NW - 1)

                guard_set(f"qh{hh}", tS)
                guard_set(f"kh{hh}", tS)

                @block.sync
                def _(sync):
                    waitm(sync, f"aoev_{h}_{NW - 1}")
                    sync.dma_start(ao_hbm.ap()[h * 128:(h + 1) * 128, :], ao_sb[hh][:]).then_inc(dSp.sem, 16)
                    dSp.n += 16
                mark(f"aospill_{h}", dSp)
                guard_set(f"ao{hh}", dSp)

            emit_wload(0)
            if ngroups > 1:
                emit_wload(1)
            for g in range(ngroups):
                if p1_upto >= 1:
                    emit_vproj(g)
                for hh in range(2):
                    if p1_upto >= 2:
                        emit_proj_rope(g, hh, "q")
                    if p1_upto >= 3:
                        emit_proj_rope(g, hh, "k")
                    if p1_upto >= 4:
                        emit_attention(g, hh)
                guard_set("v_sb", tS)
                guard_set(f"wslot{g % 2}", tS)
                if g + 2 < ngroups:
                    emit_wload(g + 2)

            if debug:
                @block.gpsimd
                def _(gp):
                    if "rope_q_0" in mil:
                        waitm(gp, "rope_q_0")
                        gp.dma_start(dbg["dbg_q0"], qh_sb[0][:]).then_inc(dX.sem, 16)
                        dX.n += 16
                        dbg_written.add("dbg_q0")
                    if "rope_k_0" in mil:
                        waitm(gp, "rope_k_0")
                        gp.dma_start(dbg["dbg_k0"], kh_sb[0][:]).then_inc(dX.sem, 16)
                        dX.n += 16
                        dbg_written.add("dbg_k0")
                    if "evv_0_17" in mil:
                        waitm(gp, "evv_0_17")
                        gp.dma_start(dbg["dbg_v0"], v_sb[:, 0:256]).then_inc(dX.sem, 16)
                        dX.n += 16
                        dbg_written.add("dbg_v0")
                    if "aoev_0_7" in mil:
                        waitm(gp, f"aospill_{2 * ngroups - 1}")
                        gp.dma_start(dbg["dbg_ao0"], ao_hbm.ap()[0:128, :]).then_inc(dX.sem, 16)
                        dX.n += 16
                        dbg_written.add("dbg_ao0")

        if phases < 2:
            endk = f"aospill_{2 * ngroups - 1}"
            if endk not in mil:
                endk = None
            @block.sync
            def _(sync):
                if endk is not None:
                    sync.wait_ge(dSp.sem, mil[endk][1])
                else:
                    sync.wait_ge(sS.sem, sS.n)
                    sync.wait_ge(vS.sem, vS.n)
                    sync.wait_ge(tS.sem, tS.n)
            finish_early(dbg_written)
            return nc, dbg

        with ExitStack() as c2:
            E2 = c2.enter_context
            wo_sb = E2(nc.sbuf_tensor("wo_sb", [128, 16 * D], BF16))
            at_sb = [E2(nc.sbuf_tensor(f"at_sb{i}", [128, 16 * 256], BF16)) for i in range(2)]
            xc_sb = [E2(nc.sbuf_tensor(f"xc_sb{i}", [128, 2 * D], F32)) for i in range(2)]
            x1s_sb = [E2(nc.sbuf_tensor(f"x1s_sb{i}", [128, D], F32)) for i in range(2)]
            sqa_sb = E2(nc.sbuf_tensor("sqa_sb", [128, D], F32))
            sqb_sb = E2(nc.sbuf_tensor("sqb_sb", [128, D], F32))
            st_sb = E2(nc.sbuf_tensor("st_sb", [128, 12], F32))
            sumsq_t = E2(nc.sbuf_tensor("sumsq_t", [128, 1], F32))
            rstd_t = E2(nc.sbuf_tensor("rstd_t", [128, 1], F32))
            m_t = E2(nc.sbuf_tensor("m_t", [128, 1], F32))
            ps_y = [E2(nc.psum_tensor(f"ps_y{i}", [128, 2048], F32)) for i in range(2)]

            @block.gpsimd
            def _(gp):
                gp.wait_ge(dSp.sem, mil["aospill_15"][1])
                for half in range(2):
                    dma_seq(gp, dW if half == 0 else dW2, wo_sb[:, half * 8 * D:(half + 1) * 8 * D]
                            .rearrange("p (a m) -> p a m", a=8),
                            wo_d[half * 1024:(half + 1) * 1024, :]
                            .rearrange("(a p) m -> p a m", p=128))
                    mark(f"wo_{half}", dW if half == 0 else dW2)

            for cch in range(8):
                @block.sync
                def _(sync, cch=cch):
                    guard_wait(sync, f"p2slot{cch % 2}")
                    sync.wait_ge(dSp.sem, mil["aospill_15"][1])
                    xctr = dX if cch % 2 == 0 else dX2
                    dma_seq(sync, xctr, at_sb[cch % 2][:].rearrange("p (a m) -> p a m", a=16),
                            ao_hbm.ap()[:, cch * 256:(cch + 1) * 256]
                            .rearrange("(a p) m -> p a m", p=128))
                    dma_seq(sync, xctr, xc_sb[cch % 2][:].rearrange("p (a m) -> p a m", a=2),
                            x_d[cch * 256:(cch + 1) * 256, :]
                            .rearrange("(a p) m -> p a m", p=128))
                mark(f"p2load_{cch}", dX if cch % 2 == 0 else dX2)

                for tt in range(2):
                    gt = cch * 2 + tt
                    py = ps_y[tt]

                    @block.tensor
                    def _(t, cch=cch, tt=tt, py=py):
                        t = TEng(t)
                        waitm(t, f"p2load_{cch}")
                        guard_wait(t, f"py{tt}")
                        for i_half in range(2):
                            waitm(t, f"wo_{i_half}")
                            for i in range(i_half * 8, i_half * 8 + 8):
                                for oc in range(4):
                                    t.matmul(py[:, oc * 512: oc * 512 + 512],
                                             at_sb[cch % 2][:, i * 256 + tt * 128: i * 256 + tt * 128 + 128],
                                             wo_sb[:, i * D + oc * 512: i * D + oc * 512 + 512],
                                             start=(i == 0), stop=(i == 15))
                        t.fin(tS.sem, 1)
                        tS.n += 1
                        mark(f"y_{gt}", tS)

                    @block.vector
                    def _(v, cch=cch, tt=tt, py=py, gt=gt):
                        v = Eng(v, vSelf)
                        waitm(v, f"y_{gt}")
                        guard_wait(v, f"x1s{gt % 2}")
                        for oc in range(4):
                            v.tensor_add(x1s_sb[gt % 2][:, oc * 512: oc * 512 + 512],
                                         py[:, oc * 512: oc * 512 + 512],
                                         xc_sb[cch % 2][:, tt * D + oc * 512: tt * D + oc * 512 + 512])
                        v.bar()
                        v.reduce_sum(st_sb[:, 0:1], x1s_sb[gt % 2][:], axis=AX.X).then_inc(vS.sem, 1)
                        vS.n += 1
                        mark(f"x1pre_{gt}", vS)
                        guard_set(f"py{tt}", vS)
                        if tt == 1:
                            guard_set(f"p2slot{cch % 2}", vS)

                    @block.scalar
                    def _(s, gt=gt):
                        s = Eng(s, sSelf)
                        waitm(s, f"x1pre_{gt}")
                        guard_wait(s, "sqa")
                        s.activation(sqa_sb[:], x1s_sb[gt % 2][:], AF.Square,
                                     accum_out=sumsq_t[:]).then_inc(sS.sem, 1)
                        sS.n += 1
                        mark(f"sumsq_{gt}", sS)

                    @block.vector
                    def _(v, gt=gt):
                        v = Eng(v, vSelf)
                        waitm(v, f"sumsq_{gt}")
                        v.tensor_scalar(m_t[:], st_sb[:, 0:1], 1.0 / D, None, op0=ALU.mult)
                        v.tensor_scalar(st_sb[:, 3:4], sumsq_t[:], 1.0 / D, None, op0=ALU.mult)
                        v.bar()
                        v.tensor_mul(st_sb[:, 4:5], m_t[:], m_t[:])
                        v.bar()
                        v.tensor_sub(st_sb[:, 5:6], st_sb[:, 3:4], st_sb[:, 4:5])
                        v.bar()
                        v.tensor_scalar(st_sb[:, 5:6], st_sb[:, 5:6], 0.0, None, op0=ALU.max).then_inc(vS.sem, 1)
                        vS.n += 1
                        mark(f"var_{gt}", vS)
                        guard_set("sqa", vS)

                    @block.scalar
                    def _(s, gt=gt):
                        s = Eng(s, sSelf)
                        waitm(s, f"var_{gt}")
                        s.activation(st_sb[:, 6:7], st_sb[:, 5:6], AF.Ln, bias=eps_c[:], scale=one_c[:])
                        s.bar()
                        s.activation(rstd_t[:], st_sb[:, 6:7], AF.Exp, bias=zero_c[:],
                                     scale=nhalf_c[:]).then_inc(sS.sem, 1)
                        sS.n += 1
                        mark(f"rstd_{gt}", sS)

                    @block.vector
                    def _(v, gt=gt):
                        v = Eng(v, vSelf)
                        waitm(v, f"rstd_{gt}")
                        v.tensor_scalar(x1s_sb[gt % 2][:], x1s_sb[gt % 2][:], m_t[:], rstd_t[:],
                                        op0=ALU.subtract, op1=ALU.mult).then_inc(vS.sem, 1)
                        vS.n += 1
                        mark(f"x1_{gt}", vS)

                    @block.vector
                    def _(v, gt=gt):
                        v = Eng(v, vSelf)
                        waitm(v, f"x1_{gt}")
                        guard_wait(v, "sqb")
                        v.tensor_mul(sqb_sb[:], x1s_sb[gt % 2][:], wr_f[:])
                        v.bar()
                        v.reduce_sum(logit_sb[:, gt:gt + 1], sqb_sb[:], axis=AX.X)
                        v.bar()
                        v.tensor_add(logit_sb[:, gt:gt + 1], logit_sb[:, gt:gt + 1],
                                     br_s[:]).then_inc(vS.sem, 1)
                        vS.n += 1
                        mark(f"lgt_{gt}", vS)
                        guard_set("sqb", vS)

                    @block.sync
                    def _(sync, gt=gt):
                        waitm(sync, f"lgt_{gt}")
                        sync.dma_start(x1_hbm.ap()[gt * 128:(gt + 1) * 128, :],
                                       x1s_sb[gt % 2][:]).then_inc(dSp.sem, 16)
                        dSp.n += 16
                    mark(f"x1spill_{gt}", dSp)
                    guard_set(f"x1s{gt % 2}", dSp)

            if debug:
                @block.gpsimd
                def _(gp):
                    waitm(gp, "x1spill_15")
                    gp.dma_start(dbg["dbg_x1a"], x1s_sb[1][:]).then_inc(dX.sem, 16)
                    gp.dma_start(dbg["dbg_logit"], logit_sb[:]).then_inc(dX.sem, 16)
                    dX.n += 32
                    dbg_written.update(["dbg_x1a", "dbg_logit"])

        if phases < 3:
            @block.sync
            def _(sync):
                sync.wait_ge(dSp.sem, mil["x1spill_15"][1])
            finish_early(dbg_written)
            return nc, dbg

        c34 = ctx.enter_context(ExitStack())
        PT_sb = c34.enter_context(nc.sbuf_tensor("PT_sb", [128, 16 * CAP], BF16))
        Pw_sb = c34.enter_context(nc.sbuf_tensor("Pw_sb", [128, 48 * 128], BF16))

        with ExitStack() as c3:
            E3 = c3.enter_context
            g8_sb = E3(nc.sbuf_tensor("g8_sb", [8, 1024], F32))
            grep_sb = E3(nc.sbuf_tensor("grep_sb", [128, 1024], F32))
            cscr_sb = E3(nc.sbuf_tensor("cscr_sb", [128, 1024], F32))
            e8_sb = E3(nc.sbuf_tensor("e8_sb", [8, 1024], F32))
            m8_sb = E3(nc.sbuf_tensor("m8_sb", [8, 1024], F32))
            sr8_sb = E3(nc.sbuf_tensor("sr8_sb", [8, 1], F32))
            cnt_sb = E3(nc.sbuf_tensor("cnt_sb", [128, 1], F32))
            tmp16_sb = E3(nc.sbuf_tensor("tmp16_sb", [128, NPROBE], F32))
            crow_sb = E3(nc.sbuf_tensor("crow_sb", [1, NPROBE], F32))
            brow_sb = E3(nc.sbuf_tensor("brow_sb", [1, 8 * NPROBE], F32))
            lh_sb = E3(nc.sbuf_tensor("lh_sb", [1, 8], F32))
            tau_sb = E3(nc.sbuf_tensor("tau_sb", [128, 1], F32))
            loc_sb = E3(nc.sbuf_tensor("loc_sb", [128, 1], F32))
            spanc_sb = E3(nc.sbuf_tensor("spanc_sb", [128, 1], F32))
            dn_sb = E3(nc.sbuf_tensor("dn_sb", [1, 2], F32))
            recd_sb = E3(nc.sbuf_tensor("recd_sb", [128, 1], F32))
            mask16_sb = E3(nc.sbuf_tensor("mask16_sb", [128, 16], F32))
            maskb_sb = E3(nc.sbuf_tensor("maskb_sb", [128, 16], BF16))
            e16_sb = E3(nc.sbuf_tensor("e16_sb", [128, 16], F32))
            rw16_sb = E3(nc.sbuf_tensor("rw16_sb", [128, 16], F32))
            q16_sb = E3(nc.sbuf_tensor("q16_sb", [128, 16], F32))
            pcol_sb = E3(nc.sbuf_tensor("pcol_sb", [1, 16], F32))
            pbc_sb = E3(nc.sbuf_tensor("pbc_sb", [128, 16], F32))
            pt_f32 = E3(nc.sbuf_tensor("pt_f32", [128, CAP], F32))
            pw_f32 = E3(nc.sbuf_tensor("pw_f32", [128, CAP], F32))
            PwT_sb = E3(nc.sbuf_tensor("PwT_sb", [128, CAP], BF16))
            qcol_t = E3(nc.sbuf_tensor("qcol_t", [128, 1], F32))
            rwcol_t = E3(nc.sbuf_tensor("rwcol_t", [128, 1], F32))
            lo_t = E3(nc.sbuf_tensor("lo_t", [1, 1], F32))
            hi_t = E3(nc.sbuf_tensor("hi_t", [1, 1], F32))
            span_t = E3(nc.sbuf_tensor("span_t", [1, 1], F32))
            red_t = E3(nc.sbuf_tensor("red_t", [1, 1], F32))
            red2_t = E3(nc.sbuf_tensor("red2_t", [1, 1], F32))

            dgR_sb = E3(nc.sbuf_tensor("dgR_sb", [128, 8], F32))
            dgG_sb = E3(nc.sbuf_tensor("dgG_sb", [128, 1024], F32))
            ps_r1 = E3(nc.psum_tensor("ps_r1", [128, 1024], F32))
            ps_r2 = E3(nc.psum_tensor("ps_r2", [1, 16], F32))
            ps_r3 = E3(nc.psum_tensor("ps_r3", [128, 16], F32))
            ps_r4 = E3(nc.psum_tensor("ps_r4", [1, 2], F32))
            ps_r5 = E3(nc.psum_tensor("ps_r5", [128, CAP], BF16))

            @block.gpsimd
            def _(gp):
                waitm(gp, "lgt_15")
                waitm(gp, "x1spill_15")
                gp.dma_start(agin.ap().rearrange("a (p t) -> p (a t)", p=128),
                             logit_sb[:]).then_inc(dSp.sem, 16)
                dSp.n += 16
                gp.wait_ge(dSp.sem, dSp.n)
                gp.collective_compute(
                    "AllGather", ALU.bypass,
                    replica_groups=[[0, 1, 2, 3], [4, 5, 6, 7]],
                    ins=[agin.ap().opt()],
                    outs=[agout.ap().opt()],
                ).then_inc(cc.sem)
                cc.n += 1
                gp.wait_ge(cc.sem, cc.n)
                dma_seq(gp, dX, g8_sb[:], agout.ap().rearrange("a (c d) -> (a c) d", d=1024))
            mark("g8", dX)

            @block.tensor
            def _(t):
                t = TEng(t)
                waitm(t, "g8")
                t.matmul(ps_r1[:, 0:512], repl_s[:], g8_sb[:, 0:512], start=True, stop=True)
                t.matmul(ps_r1[:, 512:1024], repl_s[:], g8_sb[:, 512:1024], start=True, stop=True)
                t.fin(tS.sem, 1)
                tS.n += 1
            mark("grep_mm", tS)

            @block.vector
            def _(v):
                v = Eng(v, vSelf)
                waitm(v, "grep_mm")
                v.tensor_copy(grep_sb[:], ps_r1[:])

                v.memset(lo_t[:], -64.0)
                v.memset(hi_t[:], 64.0)
                v.memset(span_t[:], 128.0).then_inc(vS.sem, 1)
                vS.n += 1
            mark("grep", vS)

            for rnd in range(NROUND):
                @block.tensor
                def _(t, rnd=rnd):
                    t = TEng(t)
                    waitm(t, f"lohi_{rnd - 1}" if rnd > 0 else "grep")
                    t.matmul(ps_r3[:, 0:1], onesr_s[:], lo_t[:], start=True, stop=True)
                    t.matmul(ps_r3[:, 1:2], onesr_s[:], span_t[:], start=True, stop=True)
                    t.fin(tS.sem, 1)
                    tS.n += 1
                    mark(f"bc_{rnd}", tS)

                @block.vector
                def _(v, rnd=rnd):
                    v = Eng(v, vSelf)
                    waitm(v, f"bc_{rnd}")
                    v.tensor_copy(loc_sb[:], ps_r3[:, 0:1])
                    v.tensor_copy(spanc_sb[:], ps_r3[:, 1:2])
                    v.bar()
                    v.tensor_mul(tau_sb[:], spanc_sb[:], stepc_s[:])
                    v.bar()
                    v.tensor_add(tau_sb[:], tau_sb[:], loc_sb[:])
                    v.bar()
                    v.tensor_scalar(cscr_sb[:], grep_sb[:], tau_sb[:], 0.0, op0=ALU.is_gt, op1=ALU.add)
                    v.bar()
                    v.reduce_sum(cnt_sb[:], cscr_sb[:], axis=AX.X)
                    v.bar()
                    v.tensor_scalar(tmp16_sb[:], bm_s[:], cnt_sb[:], None, op0=ALU.mult).then_inc(vS.sem, 1)
                    vS.n += 1
                    mark(f"cnt_{rnd}", vS)


                @block.tensor
                def _(t, rnd=rnd):
                    t = TEng(t)
                    waitm(t, f"cnt_{rnd}")
                    t.matmul(ps_r2[:, :], ones128_s[:], tmp16_sb[:], start=True, stop=True)
                    t.fin(tS.sem, 1)
                    tS.n += 1
                    mark(f"cnt2_{rnd}", tS)

                @block.vector
                def _(v, rnd=rnd):
                    v = Eng(v, vSelf)
                    waitm(v, f"cnt2_{rnd}")
                    v.tensor_copy(crow_sb[:], ps_r2[:, :])
                    v.bar()
                    P = NPROBE
                    taur = brow_sb[0:1, 0:P]
                    ge = brow_sb[0:1, P:2 * P]
                    cl = brow_sb[0:1, 2 * P:3 * P]
                    red = brow_sb[0:1, 3 * P:3 * P + 1]
                    red2 = brow_sb[0:1, 3 * P + 1:3 * P + 2]
                    ch = brow_sb[0:1, 4 * P:5 * P]
                    v.tensor_scalar(taur, stepr_s[:], span_t[:], None, op0=ALU.mult)
                    v.bar()
                    v.tensor_scalar(taur, taur, lo_t[:], None, op0=ALU.add)
                    v.bar()
                    v.tensor_scalar(ge, crow_sb[:], float(KTOP), None, op0=ALU.is_ge)
                    v.bar()
                    t2 = brow_sb[0:1, 5 * P:6 * P]
                    v.tensor_mul(cl, taur, ge)
                    v.tensor_scalar(t2, ge, 1e30, -1e30, op0=ALU.mult, op1=ALU.add)
                    v.bar()
                    v.tensor_add(cl, cl, t2)
                    v.bar()
                    v.reduce_max(red_t[:], cl, axis=AX.X)
                    v.bar()
                    v.tensor_max(lo_t[:], lo_t[:], red_t[:])
                    v.bar()
                    v.tensor_scalar(ge, ge, -1.0, 1.0, op0=ALU.mult, op1=ALU.add)
                    v.bar()
                    v.tensor_mul(ch, taur, ge)
                    v.tensor_scalar(t2, ge, -1e30, 1e30, op0=ALU.mult, op1=ALU.add)
                    v.bar()
                    v.tensor_add(ch, ch, t2)
                    v.bar()
                    v.tensor_reduce(red2_t[:], ch, axis=AX.X, op=ALU.min)
                    v.bar()
                    v.tensor_tensor(hi_t[:], hi_t[:], red2_t[:], op=ALU.min)
                    v.bar()
                    v.tensor_sub(span_t[:], hi_t[:], lo_t[:]).then_inc(vS.sem, 1)
                    vS.n += 1
                    mark(f"lohi_{rnd}", vS)
                    if debug == 2 and rnd == 0:
                        v.wait_ge(vS.sem, vS.n)
                        v.tensor_copy(dgG_sb[0:1, 0:128], brow_sb[0:1, 0:128])
                        v.tensor_copy(dgG_sb[0:1, 200:216], crow_sb[:])
                        v.tensor_copy(dgG_sb[0:1, 300:301], lo_t[:])
                        v.tensor_copy(dgG_sb[0:1, 301:302], hi_t[:])
                        v.tensor_copy(dgG_sb[0:1, 302:303], span_t[:])
                        v.tensor_copy(dgG_sb[0:1, 303:304], red_t[:])
                        v.tensor_copy(dgG_sb[0:1, 304:305], red2_t[:])
                        v.tensor_copy(dgG_sb[:, 400:401], tau_sb[:])
                        v.tensor_copy(dgG_sb[:, 401:402], cnt_sb[:]).then_inc(vS.sem, 1)
                        vS.n += 1
                        mark("dgR", vS)

            @block.tensor
            def _(t):
                t = TEng(t)
                waitm(t, f"lohi_{NROUND - 1}")
                t.matmul(ps_r3[:, 0:1], onesr_s[:], lo_t[:], start=True, stop=True)
                t.fin(tS.sem, 1)
                tS.n += 1
            mark("taubc", tS)

            @block.vector
            def _(v):
                v = Eng(v, vSelf)
                waitm(v, "taubc")
                v.tensor_copy(tau_sb[:], ps_r3[:, 0:1]).then_inc(vS.sem, 1)
                vS.n += 1
            mark("tau", vS)

            @block.scalar
            def _(s):
                s = Eng(s, sSelf)
                waitm(s, "g8")
                s.activation(e8_sb[:], g8_sb[:], AF.Exp, bias=mrs_c[0:8, :],
                             scale=one_c[0:8, :]).then_inc(sS.sem, 1)
                sS.n += 1
            mark("e8", sS)

            @block.vector
            def _(v):
                v = Eng(v, vSelf)
                waitm(v, "e8")
                waitm(v, "tau")
                v.tensor_scalar(m8_sb[:], g8_sb[:], tau_sb[0:8, :], None, op0=ALU.is_gt)
                v.bar()
                v.tensor_mul(e8_sb[:], e8_sb[:], m8_sb[:])
                v.bar()
                v.reduce_sum(sr8_sb[:], e8_sb[:], axis=AX.X).then_inc(vS.sem, 1)
                vS.n += 1
            mark("e8m", vS)

            @block.tensor
            def _(t):
                t = TEng(t)
                waitm(t, "e8m")
                t.matmul(ps_r4[:, 0:1], ones8_s[:], sr8_sb[:], start=True, stop=True)
                t.fin(tS.sem, 1)
                tS.n += 1
            mark("dn_mm", tS)

            @block.vector
            def _(v):
                v = Eng(v, vSelf)
                waitm(v, "dn_mm")
                v.tensor_copy(dn_sb[0:1, 0:1], ps_r4[:, 0:1])
                v.bar()
                v.tensor_scalar(dn_sb[0:1, 0:1], dn_sb[0:1, 0:1], 1e-30, None, op0=ALU.max)
                v.bar()
                v.reciprocal(red_t[:], dn_sb[0:1, 0:1]).then_inc(vS.sem, 1)
                vS.n += 1
            mark("dn", vS)

            @block.tensor
            def _(t):
                t = TEng(t)
                waitm(t, "dn")
                t.matmul(ps_r3[:, 2:3], onesr_s[:], red_t[:], start=True, stop=True)
                t.fin(tS.sem, 1)
                tS.n += 1
            mark("recd_mm", tS)

            @block.vector
            def _(v):
                v = Eng(v, vSelf)
                waitm(v, "recd_mm")
                v.tensor_copy(recd_sb[:], ps_r3[:, 2:3])
                v.tensor_scalar(mask16_sb[:], logit_sb[:], tau_sb[:], None, op0=ALU.is_gt)
                v.bar()
                v.tensor_copy(maskb_sb[:], mask16_sb[:]).then_inc(vS.sem, 1)
                vS.n += 1
            mark("mask16", vS)

            @block.scalar
            def _(s):
                s = Eng(s, sSelf)
                waitm(s, "mask16")
                s.activation(e16_sb[:], logit_sb[:], AF.Exp, bias=mrs_c[:],
                             scale=one_c[:]).then_inc(sS.sem, 1)
                sS.n += 1
            mark("e16", sS)

            @block.vector
            def _(v):
                v = Eng(v, vSelf)
                waitm(v, "e16")
                v.tensor_scalar(rw16_sb[:], e16_sb[:], recd_sb[:], None, op0=ALU.mult)
                v.bar()
                v.tensor_mul(rw16_sb[:], rw16_sb[:], mask16_sb[:]).then_inc(vS.sem, 1)
                vS.n += 1
            mark("rw16", vS)

            @block.tensor
            def _(t):
                t = TEng(t)
                waitm(t, "mask16")
                t.matmul(ps_r3[:, 0:16], triu_s[:], maskb_sb[:], start=True, stop=True)
                t.matmul(ps_r2[:, :], ones128_s[:], mask16_sb[:], start=True, stop=True)
                t.fin(tS.sem, 1)
                tS.n += 1
            mark("cums_mm", tS)

            @block.vector
            def _(v):
                v = Eng(v, vSelf)
                waitm(v, "cums_mm")
                v.tensor_copy(q16_sb[:], ps_r3[:, 0:16])
                v.tensor_copy(pcol_sb[:], ps_r2[:, :])
                v.bar()
                a0 = brow_sb[0:1, 0:16]
                b0 = brow_sb[0:1, 16:32]
                v.tensor_copy(a0, pcol_sb[:])
                v.bar()
                for sh in (1, 2, 4, 8):
                    v.tensor_copy(b0, a0)
                    v.bar()
                    v.tensor_add(b0[0:1, sh:16], a0[0:1, sh:16], a0[0:1, 0:16 - sh])
                    v.bar()
                    v.tensor_copy(a0, b0)
                    v.bar()
                v.tensor_sub(pcol_sb[:], a0, pcol_sb[:]).then_inc(vS.sem, 1)
                vS.n += 1
            mark("prefix", vS)

            @block.tensor
            def _(t):
                t = TEng(t)
                waitm(t, "prefix")
                t.matmul(ps_r3[:, 0:16], onesr_s[:], pcol_sb[:], start=True, stop=True)
                t.fin(tS.sem, 1)
                tS.n += 1
            mark("pfx_bc", tS)

            @block.vector
            def _(v):
                v = Eng(v, vSelf)
                waitm(v, "pfx_bc")
                v.tensor_copy(pbc_sb[:], ps_r3[:, 0:16])
                v.bar()
                v.tensor_add(q16_sb[:], q16_sb[:], pbc_sb[:])
                v.bar()
                v.tensor_mul(q16_sb[:], q16_sb[:], mask16_sb[:])
                v.tensor_copy(lh_sb[0:1, 0:1], lo_t[:])
                v.tensor_copy(lh_sb[0:1, 1:2], hi_t[:])
                v.tensor_copy(lh_sb[0:1, 2:3], span_t[:]).then_inc(vS.sem, 1)
                vS.n += 1
            mark("Q", vS)

            for i in range(16):
                @block.vector
                def _(v, i=i):
                    v = Eng(v, vSelf)
                    waitm(v, "Q")
                    waitm(v, "rw16")
                    guard_wait(v, "PwT")
                    v.tensor_copy(qcol_t[:], q16_sb[:, i:i + 1])
                    v.tensor_copy(rwcol_t[:], rw16_sb[:, i:i + 1])
                    v.bar()
                    v.tensor_scalar(pt_f32[:], iota_s[:], qcol_t[:], None, op0=ALU.is_equal)
                    v.bar()
                    v.tensor_copy(PT_sb[:, i * CAP:(i + 1) * CAP], pt_f32[:])
                    v.tensor_scalar(pw_f32[:], pt_f32[:], rwcol_t[:], None, op0=ALU.mult)
                    v.bar()
                    v.tensor_copy(PwT_sb[:], pw_f32[:]).then_inc(vS.sem, 1)
                    vS.n += 1
                    mark(f"pw_{i}", vS)

                @block.tensor
                def _(t, i=i):
                    t = TEng(t)
                    waitm(t, f"pw_{i}")
                    guard_wait(t, "psr5")
                    for jj in range(NJT):
                        t.transpose(ps_r5[:, jj * 128: jj * 128 + 128],
                                    PwT_sb[:, jj * 128: jj * 128 + 128], ident_s[:])
                    t.fin(tS.sem, 1)
                    tS.n += 1
                    mark(f"pwt_{i}", tS)
                    guard_set("PwT", tS)

                @block.vector
                def _(v, i=i):
                    v = Eng(v, vSelf)
                    waitm(v, f"pwt_{i}")
                    for jj in range(NJT):
                        v.tensor_copy(Pw_sb[:, (jj * 16 + i) * 128:(jj * 16 + i) * 128 + 128],
                                      ps_r5[:, jj * 128: jj * 128 + 128]).then_inc(vS.sem, 1)
                        vS.n += 1
                    mark(f"pwev_{i}", vS)
                    guard_set("psr5", vS)

            if debug:
                @block.gpsimd
                def _(gp):
                    if debug == 2 and "dgR" in mil:
                        waitm(gp, "dgR")
                        gp.dma_start(dbg["dbg_pT"], dgG_sb[:]).then_inc(dX.sem, 16)
                        dX.n += 16
                        dbg_written.update(["dbg_pT"])
                    waitm(gp, "pwev_15")
                    gp.dma_start(dbg["dbg_Q"], q16_sb[:]).then_inc(dX.sem, 16)
                    gp.dma_start(dbg["dbg_rw"], rw16_sb[:]).then_inc(dX.sem, 16)
                    gp.dma_start(dbg["dbg_lh"], lh_sb[:]).then_inc(dX.sem, 16)
                    gp.dma_start(dbg["dbg_cnt"], crow_sb[:]).then_inc(dX.sem, 16)
                    dX.n += 64
                    dbg_written.update(["dbg_Q", "dbg_rw", "dbg_lh", "dbg_cnt"])
            mark("p3_done", vS)

        if phases < 4:
            @block.vector
            def _(v):
                v = Eng(v, vSelf)
                waitm(v, "pwev_15")
            finish_early(dbg_written)
            return nc, dbg

        with ExitStack() as c4:
            E4 = c4.enter_context
            fxT_sb = E4(nc.sbuf_tensor("fxT_sb", [128, 16 * CAP], BF16))
            hT_sb = E4(nc.sbuf_tensor("hT_sb", [128, 64 * CAP], BF16))
            w1_sb = [E4(nc.sbuf_tensor(f"w1_sb{i}", [128, 16 * 512], BF16)) for i in range(2)]
            ol_sb = E4(nc.sbuf_tensor("ol_sb", [128, NJT * D], BF16))

            with ExitStack() as c4a:
                E4a = c4a.enter_context
                fx_sb = E4a(nc.sbuf_tensor("fx_sb", [128, NJT * D], BF16))
                x1g_f = [E4a(nc.sbuf_tensor(f"x1g_f{i}", [128, 1024], F32)) for i in range(2)]
                x1g_b = [E4a(nc.sbuf_tensor(f"x1g_b{i}", [128, 1024], BF16)) for i in range(2)]
                ps_g = [E4a(nc.psum_tensor(f"ps_g{j}", [128, 1024], F32)) for j in range(NJT)]
                ps_t4 = E4a(nc.psum_tensor("ps_t4", [128, 512], BF16))

                for half in range(2):
                    for i in range(16):
                        li = half * 16 + i

                        @block.sync
                        def _(sync, half=half, i=i, li=li):
                            if li == 0:
                                waitm(sync, "pwev_15")
                            waitm(sync, f"x1spill_{i}")
                            guard_wait(sync, f"x1g{li % 2}")
                            dma_seq(sync, dX if li % 2 == 0 else dX2, x1g_f[li % 2][:],
                                    x1_hbm.ap()[i * 128:(i + 1) * 128,
                                                half * 1024:(half + 1) * 1024])
                        mark(f"x1gl_{li}", dX if li % 2 == 0 else dX2)

                        @block.vector
                        def _(v, li=li):
                            v = Eng(v, vSelf)
                            waitm(v, f"x1gl_{li}")
                            guard_wait(v, f"x1gb{li % 2}")
                            v.tensor_copy(x1g_b[li % 2][:], x1g_f[li % 2][:]).then_inc(vS.sem, 1)
                            vS.n += 1
                            mark(f"x1gc_{li}", vS)
                            guard_set(f"x1g{li % 2}", vS)

                        @block.tensor
                        def _(t, half=half, i=i, li=li):
                            t = TEng(t)
                            waitm(t, f"x1gc_{li}")
                            waitm(t, "pwev_15")
                            if i == 0:
                                guard_wait(t, "ps_g")
                            for jj in range(NJT):
                                for dc in range(2):
                                    t.matmul(ps_g[jj][:, dc * 512: dc * 512 + 512],
                                             PT_sb[:, i * CAP + jj * 128: i * CAP + jj * 128 + 128],
                                             x1g_b[li % 2][:, dc * 512: dc * 512 + 512],
                                             start=(i == 0), stop=(i == 15))
                            t.fin(tS.sem, 1)
                            tS.n += 1
                            mark(f"fxmm_{half}_{i}", tS)
                            guard_set(f"x1gb{li % 2}", tS)

                    @block.vector
                    def _(v, half=half):
                        v = Eng(v, vSelf)
                        waitm(v, f"fxmm_{half}_15")
                        for jj in range(NJT):
                            v.tensor_copy(fx_sb[:, jj * D + half * 1024: jj * D + half * 1024 + 1024],
                                          ps_g[jj][:]).then_inc(vS.sem, 1)
                            vS.n += 1
                        mark(f"fxev_{half}", vS)
                        guard_set("ps_g", vS)

                for i in range(16):
                    @block.tensor
                    def _(t, i=i):
                        t = TEng(t)
                        waitm(t, "fxev_0")
                        waitm(t, "fxev_1")
                        guard_wait(t, "ps_t4")
                        for jj in range(NJT):
                            t.transpose(ps_t4[:, jj * 128: jj * 128 + 128],
                                        fx_sb[:, jj * D + i * 128: jj * D + i * 128 + 128], ident_s[:])
                        t.fin(tS.sem, 1)
                        tS.n += 1
                        mark(f"fxt_{i}", tS)

                    @block.vector
                    def _(v, i=i):
                        v = Eng(v, vSelf)
                        waitm(v, f"fxt_{i}")
                        for jj in range(NJT):
                            v.tensor_copy(fxT_sb[:, i * CAP + jj * 128: i * CAP + jj * 128 + 128],
                                          ps_t4[:, jj * 128: jj * 128 + 128]).then_inc(vS.sem, 1)
                            vS.n += 1
                        mark(f"fxtev_{i}", vS)
                        guard_set("ps_t4", vS)

            with ExitStack() as c4b:
                E4b = c4b.enter_context
                ps_h = [E4b(nc.psum_tensor(f"ps_h{j}", [128, CAP], F32)) for j in range(2)]

                for ch in range(16):
                    @block.gpsimd
                    def _(gp, ch=ch):
                        if ch < 2:
                            waitm(gp, "pwev_15")
                        guard_wait(gp, f"w1s{ch % 2}")
                        dma_seq(gp, dW if ch % 2 == 0 else dW2, w1_sb[ch % 2][:].rearrange("p (a m) -> p a m", a=16),
                                w1_d[:, ch * 512:(ch + 1) * 512]
                                .rearrange("(a p) m -> p a m", p=128))
                    mark(f"w1l_{ch}", dW if ch % 2 == 0 else dW2)

                    for fl in range(4):
                        ft = ch * 4 + fl

                        @block.tensor
                        def _(t, ch=ch, fl=fl, ft=ft):
                            t = TEng(t)
                            waitm(t, f"w1l_{ch}")
                            waitm(t, "fxtev_15")
                            guard_wait(t, f"ps_h{ft % 2}")
                            for i in range(16):
                                t.matmul(ps_h[ft % 2][:, 0:CAP],
                                         w1_sb[ch % 2][:, i * 512 + fl * 128: i * 512 + fl * 128 + 128],
                                         fxT_sb[:, i * CAP:(i + 1) * CAP],
                                         start=(i == 0), stop=(i == 15))
                            t.fin(tS.sem, 1)
                            tS.n += 1
                            mark(f"hmm_{ft}", tS)
                            if fl == 3:
                                guard_set(f"w1s{ch % 2}", tS)

                        @block.scalar
                        def _(s, ft=ft):
                            s = Eng(s, sSelf)
                            waitm(s, f"hmm_{ft}")
                            s.activation(hT_sb[:, ft * CAP:(ft + 1) * CAP], ps_h[ft % 2][:, 0:CAP],
                                         AF.Gelu, bias=b1_s[ft][:],
                                         scale=one_c[:]).then_inc(sS.sem, 1)
                            sS.n += 1
                            mark(f"gelu_{ft}", sS)
                            guard_set(f"ps_h{ft % 2}", sS)

                w2_sb = [E4b(nc.sbuf_tensor(f"w2_sb{i}", [128, 64 * 128], BF16)) for i in range(2)]
                olT_sb = E4b(nc.sbuf_tensor("olT_sb", [128, 16 * CAP], BF16))
                ps_t5 = E4b(nc.psum_tensor("ps_t5", [128, 512], BF16))

                for m in range(16):
                    @block.gpsimd
                    def _(gp, m=m):
                        if m < 2:
                            waitm(gp, "fxtev_15")
                        guard_wait(gp, f"w2s{m % 2}")
                        dma_seq(gp, dW if m % 2 == 0 else dW2, w2_sb[m % 2][:].rearrange("p (a m2) -> p a m2", a=64),
                                w2_d[:, m * 128:(m + 1) * 128]
                                .rearrange("(a p) m2 -> p a m2", p=128))
                    mark(f"w2l_{m}", dW if m % 2 == 0 else dW2)

                    @block.tensor
                    def _(t, m=m):
                        t = TEng(t)
                        waitm(t, f"w2l_{m}")
                        waitm(t, "gelu_63")
                        guard_wait(t, f"ps_h{m % 2}")
                        for f in range(64):
                            t.matmul(ps_h[m % 2][:, 0:CAP],
                                     w2_sb[m % 2][:, f * 128:(f + 1) * 128],
                                     hT_sb[:, f * CAP:(f + 1) * CAP],
                                     start=(f == 0), stop=(f == 63))
                        t.fin(tS.sem, 1)
                        tS.n += 1
                        mark(f"olmm_{m}", tS)
                        guard_set(f"w2s{m % 2}", tS)

                    @block.scalar
                    def _(s, m=m):
                        s = Eng(s, sSelf)
                        waitm(s, f"olmm_{m}")
                        s.activation(olT_sb[:, m * CAP:(m + 1) * CAP], ps_h[m % 2][:, 0:CAP],
                                     AF.Identity, bias=b2_s[m][:],
                                     scale=one_c[:]).then_inc(sS.sem, 1)
                        sS.n += 1
                        mark(f"olev_{m}", sS)
                        guard_set(f"ps_h{m % 2}", sS)

                for m in range(16):
                    @block.tensor
                    def _(t, m=m):
                        t = TEng(t)
                        waitm(t, f"olev_{m}")
                        guard_wait(t, "ps_t5")
                        for jj in range(NJT):
                            t.transpose(ps_t5[:, jj * 128: jj * 128 + 128],
                                        olT_sb[:, m * CAP + jj * 128: m * CAP + jj * 128 + 128],
                                        ident_s[:])
                        t.fin(tS.sem, 1)
                        tS.n += 1
                        mark(f"olt_{m}", tS)

                    @block.vector
                    def _(v, m=m):
                        v = Eng(v, vSelf)
                        waitm(v, f"olt_{m}")
                        for jj in range(NJT):
                            v.tensor_copy(ol_sb[:, jj * D + m * 128: jj * D + m * 128 + 128],
                                          ps_t5[:, jj * 128: jj * 128 + 128]).then_inc(vS.sem, 1)
                            vS.n += 1
                        mark(f"oltev_{m}", vS)
                        guard_set("ps_t5", vS)

                if debug:
                    @block.gpsimd
                    def _(gp):
                        waitm(gp, "fxev_1")
                        gp.dma_start(dbg["dbg_fx"], fx_sb[:, 0:D]).then_inc(dX.sem, 16)
                        waitm(gp, "gelu_0")
                        gp.dma_start(dbg["dbg_h0"], hT_sb[:, 0:CAP]).then_inc(dX.sem, 16)
                        waitm(gp, "oltev_15")
                        gp.dma_start(dbg["dbg_ol"], ol_sb[:, 0:D]).then_inc(dX.sem, 16)
                        dX.n += 48
                        dbg_written.update(["dbg_fx", "dbg_h0", "dbg_ol"])

            x1r_sb = [E4(nc.sbuf_tensor(f"x1r_sb{i}", [128, D], F32)) for i in range(2)]
            o_sb = [E4(nc.sbuf_tensor(f"o_sb{i}", [128, D], F32)) for i in range(2)]
            sq4_sb = E4(nc.sbuf_tensor("sq4_sb", [128, D], F32))
            st4_sb = E4(nc.sbuf_tensor("st4_sb", [128, 12], F32))
            sumsq4_t = E4(nc.sbuf_tensor("sumsq4_t", [128, 1], F32))
            rstd4_t = E4(nc.sbuf_tensor("rstd4_t", [128, 1], F32))
            m4_t = E4(nc.sbuf_tensor("m4_t", [128, 1], F32))
            ps_d4 = [E4(nc.psum_tensor(f"ps_d4{i}", [128, 2048], F32)) for i in range(2)]

            for gt in range(16):
                @block.sync
                def _(sync, gt=gt):
                    if gt < 2:
                        waitm(sync, "oltev_15")
                    guard_wait(sync, f"x1r{gt % 2}")
                    dma_seq(sync, dX if gt % 2 == 0 else dX2, x1r_sb[gt % 2][:],
                            x1_hbm.ap()[gt * 128:(gt + 1) * 128, :])
                mark(f"x1rl_{gt}", dX if gt % 2 == 0 else dX2)

                @block.tensor
                def _(t, gt=gt):
                    t = TEng(t)
                    waitm(t, "oltev_15")
                    guard_wait(t, f"ps_d4{gt % 2}")
                    for dc in range(4):
                        for jj in range(NJT):
                            t.matmul(ps_d4[gt % 2][:, dc * 512: dc * 512 + 512],
                                     Pw_sb[:, (jj * 16 + gt) * 128:(jj * 16 + gt) * 128 + 128],
                                     ol_sb[:, jj * D + dc * 512: jj * D + dc * 512 + 512],
                                     start=(jj == 0), stop=(jj == NJT - 1))
                    t.fin(tS.sem, 1)
                    tS.n += 1
                    mark(f"dmm_{gt}", tS)

                @block.vector
                def _(v, gt=gt):
                    v = Eng(v, vSelf)
                    waitm(v, f"dmm_{gt}")
                    waitm(v, f"x1rl_{gt}")
                    guard_wait(v, f"o{gt % 2}")
                    for dc in range(4):
                        v.tensor_add(o_sb[gt % 2][:, dc * 512: dc * 512 + 512],
                                     ps_d4[gt % 2][:, dc * 512: dc * 512 + 512],
                                     x1r_sb[gt % 2][:, dc * 512: dc * 512 + 512])
                    v.bar()
                    v.reduce_sum(st4_sb[:, 0:1], o_sb[gt % 2][:], axis=AX.X).then_inc(vS.sem, 1)
                    vS.n += 1
                    mark(f"x2_{gt}", vS)
                    guard_set(f"ps_d4{gt % 2}", vS)
                    guard_set(f"x1r{gt % 2}", vS)

                @block.scalar
                def _(s, gt=gt):
                    s = Eng(s, sSelf)
                    waitm(s, f"x2_{gt}")
                    guard_wait(s, "sq4")
                    s.activation(sq4_sb[:], o_sb[gt % 2][:], AF.Square,
                                 accum_out=sumsq4_t[:]).then_inc(sS.sem, 1)
                    sS.n += 1
                    mark(f"sumsq4_{gt}", sS)

                @block.vector
                def _(v, gt=gt):
                    v = Eng(v, vSelf)
                    waitm(v, f"sumsq4_{gt}")
                    v.tensor_scalar(m4_t[:], st4_sb[:, 0:1], 1.0 / D, None, op0=ALU.mult)
                    v.tensor_scalar(st4_sb[:, 3:4], sumsq4_t[:], 1.0 / D, None, op0=ALU.mult)
                    v.bar()
                    v.tensor_mul(st4_sb[:, 4:5], m4_t[:], m4_t[:])
                    v.bar()
                    v.tensor_sub(st4_sb[:, 5:6], st4_sb[:, 3:4], st4_sb[:, 4:5])
                    v.bar()
                    v.tensor_scalar(st4_sb[:, 5:6], st4_sb[:, 5:6], 0.0, None, op0=ALU.max).then_inc(vS.sem, 1)
                    vS.n += 1
                    mark(f"var4_{gt}", vS)
                    guard_set("sq4", vS)

                @block.scalar
                def _(s, gt=gt):
                    s = Eng(s, sSelf)
                    waitm(s, f"var4_{gt}")
                    s.activation(st4_sb[:, 6:7], st4_sb[:, 5:6], AF.Ln, bias=eps_c[:], scale=one_c[:])
                    s.bar()
                    s.activation(rstd4_t[:], st4_sb[:, 6:7], AF.Exp, bias=zero_c[:],
                                 scale=nhalf_c[:]).then_inc(sS.sem, 1)
                    sS.n += 1
                    mark(f"rstd4_{gt}", sS)

                @block.vector
                def _(v, gt=gt):
                    v = Eng(v, vSelf)
                    waitm(v, f"rstd4_{gt}")
                    v.tensor_scalar(o_sb[gt % 2][:], o_sb[gt % 2][:], m4_t[:], rstd4_t[:],
                                    op0=ALU.subtract, op1=ALU.mult).then_inc(vS.sem, 1)
                    vS.n += 1
                    mark(f"o_{gt}", vS)

                @block.sync
                def _(sync, gt=gt):
                    waitm(sync, f"o_{gt}")
                    sync.dma_start(out_d[gt * 128:(gt + 1) * 128, :],
                                   o_sb[gt % 2][:]).then_inc(dO.sem, 16)
                    dO.n += 16
                mark(f"ostore_{gt}", dO)
                guard_set(f"o{gt % 2}", dO)

            @block.sync
            def _(sync):
                sync.wait_ge(dO.sem, dO.n)

    return nc, dbg


def _stage_inputs(inputs):
    x = _f32(inputs["L0_attn_x"])
    Wq = _f32(inputs["L0_attn_Wq"])
    Wk = _f32(inputs["L0_attn_Wk"])
    Wv = _f32(inputs["L0_attn_Wv"])
    Wo = _f32(inputs["L0_attn_Wo"])
    Wr = _f32(inputs["L0_ffn_Wr"])
    br = _f32(inputs["L0_ffn_br"])
    W1 = _f32(inputs["L0_ffn_W1"])
    b1 = _f32(inputs["L0_ffn_b1"])
    W2 = _f32(inputs["L0_ffn_W2"])
    b2 = _f32(inputs["L0_ffn_b2"])

    perm = np.concatenate([hh * DH + np.concatenate([np.arange(0, DH, 2), np.arange(1, DH, 2)])
                           for hh in range(H)])
    wqT = _f32(Wq[perm, :].T)
    wkT = _f32(Wk[perm, :].T)
    wvT = _f32(Wv.T)
    woT = _f32(Wo.T)

    wr_rep = _f32(np.broadcast_to(Wr[:, 0][None, :], (128, D)))
    b1c = _f32(b1.reshape(DFF // 128, 128))
    b2c = _f32(b2.reshape(D // 128, 128))
    br_rep = _f32(np.broadcast_to(br.reshape(1, 1), (128, 1)))

    ident_b = _bf(np.eye(128))
    psw = np.zeros((128, 128), np.float32)
    for p in range(128):
        psw[(p + 64) % 128, p] = 1.0
    psw_b = _bf(psw)
    rr = np.arange(128)
    tri = np.where(rr[None, :] <= rr[:, None], 0.0, -1e30).astype(np.float32)
    trimask_b = _bf(tri)
    triu_b = _bf(np.triu(np.ones((128, 128), np.float32)))
    iota384 = _f32(np.broadcast_to(np.arange(1, CAP + 1, dtype=np.float32)[None, :], (128, CAP)))
    bm16 = _f32((rr[:, None] // 8 == np.arange(NPROBE)[None, :]).astype(np.float32))
    step_col = _f32(((rr // 8 + 1) / (NPROBE + 1.0))[:, None])
    step_row = _f32(((np.arange(NPROBE) + 1) / (NPROBE + 1.0))[None, :])
    repl8 = _f32((np.arange(8)[:, None] == (rr[None, :] % 8)).astype(np.float32))
    ones8 = _f32(np.ones((8, 1)))
    ones128 = _f32(np.ones((128, 1)))
    onesr = _f32(np.ones((1, 128)))

    inv = 1.0 / (10000.0 ** (np.arange(0, DH, 2, dtype=np.float32) / DH))
    s128 = 1.0 / np.sqrt(DH)

    per_core = []
    for c in range(NCORE):
        b = c // 4
        qd = c % 4
        t0 = qd * T_CORE
        xc = _f32(x[b, t0:t0 + T_CORE, :])
        xTfull = np.ascontiguousarray(x[b].T)
        if qd == 0:
            halo = np.zeros((D, HALO), np.float32)
        else:
            halo = xTfull[:, t0 - HALO:t0]
        xTc = _f32(np.concatenate([halo, xTfull[:, t0:t0 + T_CORE]], axis=1))

        tq = t0 + np.arange(T_CORE, dtype=np.float64)
        tk = np.maximum(t0 - HALO + np.arange(T_K, dtype=np.float64), 0.0)
        angq = np.outer(inv.astype(np.float64), tq)
        angk = np.outer(inv.astype(np.float64), tk)
        c2q = _bf(np.concatenate([np.cos(angq), np.cos(angq)], axis=0) * s128)
        s2q = _bf(np.concatenate([-np.sin(angq), np.sin(angq)], axis=0) * s128)
        c2k = _bf(np.concatenate([np.cos(angk), np.cos(angk)], axis=0))
        s2k = _bf(np.concatenate([-np.sin(angk), np.sin(angk)], axis=0))

        w0prev = (np.zeros((128, 256), np.float32) if qd != 0
                  else np.full((128, 256), -1e30, np.float32))

        per_core.append({
            "x": xc, "xT": xTc, "wqT": wqT, "wkT": wkT, "wvT": wvT, "woT": woT,
            "w1": W1, "w2": W2, "wr_rep": wr_rep, "b1c": b1c, "b2c": b2c, "br_rep": br_rep,
            "ident_b": ident_b, "psw_b": psw_b, "trimask_b": trimask_b,
            "w0prev_b": _bf(w0prev), "triu_b": triu_b, "iota384": iota384, "bm16": bm16,
            "step_col": step_col, "step_row": step_row, "repl8": repl8, "ones8": ones8,
            "ones128": ones128, "onesr_f32": onesr,
            "c2q": c2q, "s2q": s2q, "c2k": c2k, "s2k": s2k,
        })
    return per_core


def kernel(**inputs):
    per_core = _stage_inputs(inputs)
    nc, _ = build(debug=False)
    res = run_bass_kernel_spmd(nc, per_core, core_ids=list(range(NCORE)))
    full = np.empty((2, 8192, D), np.float32)
    for c in range(NCORE):
        full[c // 4, (c % 4) * T_CORE:(c % 4 + 1) * T_CORE, :] = res.results[c]["out"]
    return full
